# version 14
# baseline (speedup 1.0000x reference)
import contextlib
import numpy as np
import ml_dtypes
import concourse.bass as bass
import concourse.mybir as mybir
from concourse.bass_utils import run_bass_kernel_spmd

F32 = mybir.dt.float32
BF16 = mybir.dt.bfloat16
AX = mybir.AxisListType
ALU = mybir.AluOpType
ACT = mybir.ActivationFunctionType

DEBUG_DUMP = False
NPHASES = 4

S, D, H, HD, HF = 4096, 2048, 16, 128, 5632
CH = 512
NCH = S // CH
RB = 512
NRB = S // RB
NCORES = 8
QSCALE = float(1.0 / np.sqrt(128.0))
EPS_LN = 1e-5
EPS_Q = 1e-8
NW = float(HF * D)
NEG = -1e9


def latest(toks):
    d = {}
    for c, v in toks:
        k = id(c)
        if k not in d or d[k][1] < v:
            d[k] = (c, v)
    return list(d.values())


def ops_end(nc, refs, ops, Q_S, Q_C, Q_G, se, emit):
    se(Q_C.tokens(), lambda e: e.nop(), None)
    emit("sync", Q_S.tokens(), lambda e: e.nop(), None)
    emit("gpsimd", Q_G.tokens(), lambda e: e.nop(), None)
    import contextlib as _ctx
    with _ctx.ExitStack() as stack:
        for name, ref in refs.items():
            ref.sem = stack.enter_context(nc.semaphore(name))
        with nc.Block() as block:
            @block.sync
            def _(e):
                for fn in ops["sync"]:
                    fn(e)

            @block.gpsimd
            def _(e):
                for fn in ops["gpsimd"]:
                    fn(e)

            @block.vector
            def _(e):
                for fn in ops["vector"]:
                    fn(e)

            @block.scalar
            def _(e):
                for fn in ops["scalar"]:
                    fn(e)

            @block.tensor
            def _(e):
                for fn in ops["tensor"]:
                    fn(e)


def build_nc(debug=False):
    nc = bass.Bass("TRN2", target_bir_lowering=False)
    scratch_kind = "ExternalOutput" if debug else "Internal"

    x_h = nc.dram_tensor("x", [S, D], F32, kind="ExternalInput").ap()
    win = {w: nc.dram_tensor(w, [D, D], F32, kind="ExternalInput").ap()
           for w in ("wq", "wk", "wv", "wo")}
    w1_h = nc.dram_tensor("w1", [HF, D], F32, kind="ExternalInput").ap()
    w2_h = nc.dram_tensor("w2", [HF, D], F32, kind="ExternalInput").ap()
    w3_h = nc.dram_tensor("w3", [D, HF], F32, kind="ExternalInput").ap()
    ident_h = nc.dram_tensor("ident", [128, 128], BF16, kind="ExternalInput").ap()
    ident4_h = nc.dram_tensor("ident4", [128, 512], BF16, kind="ExternalInput").ap()
    mask4b_h = nc.dram_tensor("mask4b", [128, 512], BF16,
                              kind="ExternalInput").ap()
    out_h = nc.dram_tensor("out", [S, D], F32, kind="ExternalOutput").ap()

    wT = {w: nc.dram_tensor(f"wT_{w}", [H, 128, D], BF16, kind="Internal").ap()
          for w in ("q", "k", "v")}
    woT_h = nc.dram_tensor("woT", [16, 128, D], BF16, kind="Internal").ap()
    w1T_h = nc.dram_tensor("w1T", [44, 128, 2048], BF16, kind=scratch_kind).ap()
    w2T_h = nc.dram_tensor("w2T", [44, 128, 2048], BF16, kind="Internal").ap()
    w3T_h = nc.dram_tensor("w3T", [4, 128, 44 * 512], BF16,
                           kind=scratch_kind).ap()
    ln1_hbm = nc.dram_tensor("ln1s", [S, D], BF16, kind=scratch_kind).ap()
    A_hbm = nc.dram_tensor("As", [32, 128, 2048], BF16, kind=scratch_kind).ap()
    x2_hbm = nc.dram_tensor("x2s", [S, D], F32, kind=scratch_kind).ap()

    NBF = 78592
    NFA = 10496
    BF = nc.alloc_sbuf_tensor("bfar", [128, NBF], BF16,
                              is_transpose_dest=True).ap()
    FA = nc.alloc_sbuf_tensor("f32ar", [128, NFA], F32).ap()
    assert nc.sbuf_bytes_remaining >= 0

    ident = BF[:, 0:128]
    ident4 = BF[:, 128:640]
    sigb = FA[:, 0:512]
    ones = FA[:, 512:640]
    gtmp = FA[:, 640:768]

    ps = [nc.alloc_psum_tensor(f"ps{i}", [128, 512], F32).ap() for i in range(8)]

    ops = {k: [] for k in ("tensor", "vector", "scalar", "sync", "gpsimd")}
    last_wait = {k: {} for k in ops}

    class SemRef:
        sem = None

    class Ctr:
        def __init__(self, ref, step):
            self.ref, self.n, self.step = ref, 0, step

        @property
        def sem(self):
            return self.ref.sem

        def inc(self):
            self.n += self.step
            return self.n

    NQ = 4
    refs = {}

    def _mkctr(name, step):
        refs[name] = SemRef()
        return Ctr(refs[name], step)

    class Queue:
        def __init__(self, name, n=NQ):
            self.slots = [_mkctr(f"{name}{i}", 16) for i in range(n)]
            self.i = 0

        def next_slot(self):
            s = self.slots[self.i % len(self.slots)]
            self.i += 1
            return s

        def tokens(self):
            return [(s, s.n) for s in self.slots]

    Q_S = Queue("dS")
    Q_C = Queue("dC")
    Q_G = Queue("dG", 4)
    T = _mkctr("T", 1)
    V = _mkctr("V", 1)
    C = _mkctr("C", 1)

    def emit(eng, waits, build, inc=None, fresh=False):
        if eng == "vector":
            waits = list(waits) + [(V, V.n)]
        elif eng == "scalar":
            waits = list(waits) + [(C, C.n)]
        ws = []
        for ctr, val in waits:
            if ctr is None or val <= 0:
                continue
            key = id(ctr)
            if not (fresh and ctr in (T, V, C)):
                if last_wait[eng].get(key, -1) >= val:
                    continue
            if last_wait[eng].get(key, -1) < val:
                last_wait[eng][key] = val
            ws.append((ctr, val))
        if inc is not None:
            inc.inc()

        def step(e, ws=ws, build=build, inc=inc):
            for ctr, val in ws:
                e.wait_ge(ctr.sem, val)
            ins = build(e)
            if inc is not None:
                if ins is None:
                    e.nop().then_inc(inc.sem, inc.step)
                else:
                    ins.then_inc(inc.sem, inc.step)
        ops[eng].append(step)

    def te(waits, build, inc=None):
        emit("tensor", waits, build, inc)

    def ve(waits, build, inc=None):
        emit("vector", waits, build, inc)

    def se(waits, build, inc=None):
        emit("scalar", waits, build, inc)

    EMIT = {"vector": ve, "scalar": se}
    ECTR = {"vector": V, "scalar": C}
    QENG = {"sync": Q_S, "scalar": Q_C, "gpsimd": Q_G}

    def dma(eng, waits, build):
        q = QENG[eng]
        slot = q.next_slot()
        emit(eng, list(waits) + [(slot, slot.n)], build, inc=slot, fresh=True)
        return (slot, slot.n)

    def stat(si):
        b = 8 * si
        return {"sum": gtmp[:, b:b + 1], "sq": gtmp[:, b + 1:b + 2],
                "m": gtmp[:, b + 2:b + 3], "t1": gtmp[:, b + 3:b + 4],
                "nv": gtmp[:, b + 4:b + 5], "r": gtmp[:, b + 5:b + 6],
                "mr": gtmp[:, b + 6:b + 7], "msq": gtmp[:, b + 7:b + 8],
                "sq4": gtmp[:, 40 + 4 * si:44 + 4 * si]}
    ST0 = stat(0)
    nmx4 = [gtmp[:, 8:12], gtmp[:, 56:60], gtmp[:, 20:24], gtmp[:, 24:28]]
    sums4 = [gtmp[:, 12:16], gtmp[:, 60:64]]
    rec4 = [gtmp[:, 16:20], gtmp[:, 64:68]]
    ln4_scol = 60
    ln4_qcol = 76
    trashF4 = gtmp[:, 108:112]
    thrn = gtmp[:, 112:115]
    thr = gtmp[:, 116:119]
    c_1oD = gtmp[:, 122:123]
    c_eps = gtmp[:, 124:125]
    c_nhalf = gtmp[:, 125:126]
    c_none = gtmp[:, 126:127]
    c_qs = gtmp[:, 127:128]

    from concourse.hw_specs import get_activation_tables
    _tables = list(get_activation_tables(nc.m.arch).keys())

    def act_load(set_name):
        idx = _tables.index(set_name)

        def b(e, idx=idx):
            return e.add_instruction(mybir.InstLoadActFuncSet(
                name=nc.get_next_instruction_name(),
                act_func_set_id=idx, ins=[], outs=[]))
        se([], b, None)

    def emit_ln(src, width, trash, extra_v=(), extra_c=()):
        assert width == 2048
        ve(list(extra_v), lambda e: e.tensor_reduce(
            ST0["sum"], src, AX.X, ALU.add), V)
        ve([], lambda e: e.tensor_scalar(ST0["m"], ST0["sum"], 1.0 / width,
                                         None, ALU.mult), V)
        m_v = V.n
        for i in range(4):
            se(list(extra_c) if i == 0 else [],
               lambda e, i=i: e.activation(
                   trash, src[:, i * 512:(i + 1) * 512],
                   ACT.Square, accum_out=ST0["sq4"][:, i:i + 1]), C)
        se([], lambda e: e.activation(
            trashF4, ST0["sq4"], ACT.Copy, accum_out=ST0["sq"]), C)
        se([(V, m_v)], lambda e: e.activation(ST0["msq"], ST0["m"],
                                              ACT.Square), C)
        se([], lambda e: e.activation(ST0["nv"], ST0["msq"], ACT.Copy,
                                      scale=-1.0, bias=EPS_LN), C)
        se([], lambda e: e.activation(
            ST0["t1"], ST0["sq"], ACT.Ln, scale=1.0 / D, bias=ST0["nv"]), C)
        se([], lambda e: e.activation(ST0["r"], ST0["t1"], ACT.Exp,
                                      scale=c_nhalf), C)
        return C.n

    def emit_ln_v(src, si, trash, extra_v=(), extra_c=()):
        st = stat(si)
        se(list(extra_c), lambda e: e.activation(
            trash, src, ACT.Square, accum_out=st["sq"]), C)
        sq_c = C.n
        ve(list(extra_v), lambda e: e.tensor_reduce(
            st["sum"], src, AX.X, ALU.add), V)
        ve([], lambda e: e.tensor_scalar(st["m"], st["sum"], 1.0 / D, None,
                                         ALU.mult), V)
        ve([], lambda e: e.tensor_scalar(st["t1"], st["m"], st["m"], None,
                                         ALU.mult), V)
        ve([], lambda e: e.tensor_scalar(st["nv"], st["t1"], -1.0, EPS_LN,
                                         ALU.mult, ALU.add), V)
        ve([(C, sq_c)], lambda e: e.tensor_scalar(
            st["msq"], st["sq"], 1.0 / D, None, ALU.mult), V)
        ve([], lambda e: e.tensor_tensor(st["t1"], st["msq"], st["nv"],
                                         ALU.add), V)
        ve([], lambda e: e.reciprocal(st["nv"], st["t1"]), V)
        v_done = V.n
        se([(V, v_done)],
           lambda e: e.activation(st["r"], st["nv"], ACT.Sqrt), C)
        return v_done, C.n

    xbuf = [FA[:, 768 + i * 2048: 768 + (i + 1) * 2048] for i in range(2)]
    smask = [FA[:, 4864 + i * 512: 4864 + (i + 1) * 512] for i in range(2)]
    trashA = FA[:, 5888:6400]
    wload_g = [FA[:, 6400 + i * 1024: 6400 + (i + 1) * 1024] for i in range(4)]

    ln1b = [BF[:, 640 + i * 2048: 640 + (i + 1) * 2048] for i in range(2)]
    ln1T = [BF[:, 4736 + i * 8192: 4736 + (i + 1) * 8192] for i in range(2)]
    wstream = [BF[:, 21120 + i * 2048: 21120 + (i + 1) * 2048]
               for i in range(3)]
    QT = BF[:, 27264:35456]
    KT = BF[:, 35456:43648]
    VT = BF[:, 43648:51840]
    Abuf = BF[:, 51840:60032]
    expS = [BF[:, 60032 + i * 512: 60032 + (i + 1) * 512] for i in range(2)]
    attnT_sb = [BF[:, 61056 + i * 512: 61056 + (i + 1) * 512]
                for i in range(2)]
    diag4 = [BF[:, 62080 + i * 512: 62080 + (i + 1) * 512] for i in range(2)]
    mask4bf = BF[:, 63104:63616]
    wload_q = [BF[:, 63616 + i * 2048: 63616 + (i + 1) * 2048].bitcast(F32)
               for i in range(2)]
    lbuf = BF[:, 67712:69760].bitcast(F32)
    qstage = [BF[:, 69760 + i * 1024: 69760 + (i + 1) * 1024]
              for i in range(2)]
    tpstage = [BF[:, 71808:72320], BF[:, 72320:72832], BF[:, 77952:78464]]

    consts_toks = [
        dma("sync", [], lambda e: e.dma_start(ident, ident_h)),
        dma("sync", [], lambda e: e.dma_start(ident4, ident4_h)),
        dma("sync", [], lambda e: e.dma_start(mask4bf, mask4b_h)),
    ]
    act_load("natural_log_exp_and_others")
    ve([], lambda e: e.memset(ones, 1.0), V)
    ve([], lambda e: e.memset(c_eps, EPS_LN), V)
    ve([], lambda e: e.memset(c_nhalf, -0.5), V)
    ve([], lambda e: e.memset(c_none, -1.0), V)
    ve([], lambda e: e.memset(c_qs, QSCALE), V)
    ve([], lambda e: e.memset(c_1oD, 1.0 / D), V)
    nc.const_aps.register(F32, EPS_LN, c_eps)
    nc.const_aps.register(F32, -1.0, c_none)
    nc.const_aps.register(F32, 1.0 / D, c_1oD)
    ones_done = V.n

    bank_free = {i: None for i in range(8)}

    g_tiles = []
    for wi, (srcw, nt, wdt) in enumerate(((w1_h, 44, 2048), (w2_h, 44, 2048),
                                          (w3_h, 16, 5632))):
        for t in range(nt):
            for c0 in range(0, wdt, 1024):
                cw = min(1024, wdt - c0)
                g_tiles.append((wi, srcw, t, c0, cw))
    g_last = {}
    for i, g in enumerate(g_tiles):
        g_last[g[0]] = i
    g_state = {"i": 0, "cons": [0, 0, 0, 0],
               "first": {0: True, 1: True, 2: True}, "done_v": {}}
    thr_state = {"done": {0: False, 1: False, 2: False}, "v": {}}

    def emit_gamma_step():
        if g_state["i"] >= len(g_tiles):
            return False
        wi, srcw, t, c0, cw = g_tiles[g_state["i"]]
        b = g_state["i"] % 4
        idx = g_state["i"]
        g_state["i"] += 1
        eng = "gpsimd" if idx % 2 == 0 else "scalar"
        tok = dma(eng, [(V, g_state["cons"][b])],
                  lambda e, d=wload_g[b], s=srcw, t=t, c0=c0, cw=cw:
                  e.dma_start(d[:, 0:cw],
                              s[t * 128:(t + 1) * 128, c0:c0 + cw]))
        acol = 96 + wi
        if g_state["first"][wi]:
            g_state["first"][wi] = False
            ve([tok], lambda e, s=wload_g[b], cw=cw, a=acol: e.tensor_reduce(
                gtmp[:, a:a + 1], s[:, 0:cw], AX.X, ALU.add,
                apply_absolute_value=True), V)
        else:
            ve([tok], lambda e, s=wload_g[b], cw=cw, a=acol: e.tensor_reduce(
                gtmp[:, a + 8:a + 9], s[:, 0:cw], AX.X, ALU.add,
                apply_absolute_value=True), V)
            ve([], lambda e, a=acol: e.tensor_tensor(
                gtmp[:, a:a + 1], gtmp[:, a:a + 1], gtmp[:, a + 8:a + 9],
                ALU.add), V)
        g_state["cons"][b] = V.n
        if idx == g_last[wi]:
            g_state["done_v"][wi] = V.n
        return True

    def emit_thr_finalize(wi):
        acol = 96 + wi
        te([(V, g_state["done_v"][wi]), (V, ones_done)]
           + (bank_free[4] or []),
           lambda e, a=acol: e.matmul(ps[4][0:1, a - 96:a - 95], ones[:, 0:1],
                                      gtmp[:, a:a + 1],
                                      start=True, stop=True), T)
        thr_row = gtmp[0:1, 92 + wi:93 + wi]
        ve([(T, T.n)],
           lambda e, d=thr_row, wi=wi: e.tensor_scalar(
               d, ps[4][0:1, wi:wi + 1], 0.5 / NW, 0.5 * EPS_Q,
               ALU.mult, ALU.add), V)
        te([(V, V.n)],
           lambda e, s=thr_row, wi=wi: e.matmul(
               ps[4][0:128, 8 + wi:9 + wi], ones[0:1, 0:128], s,
               start=True, stop=True), T)
        ve([(T, T.n)], lambda e, wi=wi: e.tensor_copy(
            thr[:, wi:wi + 1], ps[4][:, 8 + wi:9 + wi]), V)
        ve([], lambda e, wi=wi: e.tensor_scalar(
            thrn[:, wi:wi + 1], thr[:, wi:wi + 1], -1.0, None, ALU.mult), V)
        thr_state["v"][wi] = V.n
        bank_free[4] = [(V, V.n)]

    q_tiles = []
    for wi, (srcw, nt, wdt) in enumerate(((w1_h, 44, 2048), (w2_h, 44, 2048),
                                          (w3_h, 16, 5632))):
        for t in range(nt):
            for c0 in range(0, wdt, 1024):
                cw = min(1024, wdt - c0)
                q_tiles.append((wi, srcw, t, c0, cw))
    q_state = {"i": 0, "cons": [0, 0, 0, 0, 0, 0], "tp": 0,
               "tp_cons": [None, None, None],
               "mm": [0, 0, 0, 0, 0, 0], "first_gate": [],
               "store_toks": {0: [], 1: [], 2: []}}

    def emit_quant_step():
        if q_state["i"] >= len(q_tiles):
            return False
        wi, srcw, t, c0, cw = q_tiles[q_state["i"]]
        if not thr_state["done"][wi]:
            return False
        if q_state["i"] == 0:
            assert wunit_state["i"] == len(units)
            assert g_state["i"] == len(g_tiles)
            fcmax = max(fwl_cons)
            q_state["cons"] = [fcmax, fcmax] + list(g_state["cons"])
            q_state["first_gate"] = latest(fwl_toks)
        b = q_state["i"] % 6
        q_state["i"] += 1
        wld = (wload_q + wload_g)[b]
        tok = dma("gpsimd", [(V, q_state["cons"][b])],
                  lambda e, d=wld, s=srcw, t=t, c0=c0, cw=cw:
                  e.dma_start(d[:, 0:cw],
                              s[t * 128:(t + 1) * 128, c0:c0 + cw]))
        fg = q_state["first_gate"]
        q_state["first_gate"] = []
        qb = q_state["i"] % 2
        ve([tok, (V, thr_state["v"][wi])] + fg,
           lambda e, s=wld, cw=cw, wi=wi: e.tensor_scalar(
               lbuf[:, 0:cw], s[:, 0:cw], thrn[:, wi:wi + 1], None,
               ALU.is_le), V)
        ve([(T, q_state["mm"][qb])],
           lambda e, s=wld, q=qstage[qb], cw=cw, wi=wi:
           e.scalar_tensor_tensor(q[:, 0:cw], s[:, 0:cw],
                                  thr[:, wi:wi + 1], lbuf[:, 0:cw],
                                  ALU.is_ge, ALU.subtract), V)
        q_state["cons"][b] = V.n
        stt_v = V.n
        for g in range(cw // 512):
            bank = 2 + q_state["tp"] % 2
            tb = q_state["tp"] % 3
            q_state["tp"] += 1
            tw = [(V, stt_v)] + (bank_free[bank] or []) + consts_toks
            for i in range(4):
                te(tw if i == 0 else [],
                   lambda e, bk=bank, q=qstage[qb], g=g, i=i: e.matmul(
                       ps[bk][:, i * 128:(i + 1) * 128],
                       q[:, g * 512 + i * 128: g * 512 + (i + 1) * 128],
                       ident, start=True, stop=True),
                   T if i == 3 else None)
            mm = T.n
            q_state["mm"][qb] = mm
            se([(T, mm)] + ([q_state["tp_cons"][tb]]
                            if q_state["tp_cons"][tb] else []),
               lambda e, bk=bank, tb=tb: e.copy(tpstage[tb], ps[bk][:]), C)
            bank_free[bank] = [(C, C.n)]
            if wi < 2:
                wTd = w1T_h if wi == 0 else w2T_h
                stok = dma("scalar", [(C, C.n)],
                           lambda e, d=wTd, t=t, c0=c0, g=g, tb=tb:
                           e.dma_start(
                               d[t][:, c0 + g * 512: c0 + (g + 1) * 512],
                               tpstage[tb]))
            else:
                ccb = (c0 + g * 512) // 128
                zoff = (t % 4) * 128
                stok = dma(
                    "scalar", [(C, C.n)],
                    lambda e, t=t, ccb=ccb, zoff=zoff, tb=tb: e.dma_start(
                        w3T_h[t // 4].rearrange("p (cc z) -> p cc z", z=512)[
                            :, ccb:ccb + 4, zoff:zoff + 128],
                        tpstage[tb].rearrange("p (i z) -> p i z", i=4)))
            q_state["tp_cons"][tb] = stok
            q_state["store_toks"][wi].append(stok)
        return True

    def emit_prep_light():
        if emit_gamma_step():
            emit_gamma_step()
            return True
        return False

    def emit_prep_step():
        if emit_prep_light():
            return True
        for wi in range(3):
            if (wi in g_state["done_v"] and not thr_state["done"][wi]):
                emit_thr_finalize(wi)
                thr_state["done"][wi] = True
                return True
        return emit_quant_step()

    fwl = [BF[:, 63616 + i * 2048: 63616 + (i + 1) * 2048].bitcast(F32)
           for i in range(4)]
    fwl_cons = [0, 0, 0, 0]
    fwl_toks = []
    wst_cons = [0, 0, 0]
    wT_gate = {w: {} for w in ("q", "k", "v", "o")}
    units = [(w, t) for w in ("q", "k", "v", "o") for t in range(16)]
    for u, (w, t) in enumerate(units):
        cb_ = u % 3
        src = win["w" + w]
        for hf in range(2):
            fb = (2 * u + hf) % 4
            ltok = dma("sync" if hf == 0 else "gpsimd",
                       [(V, fwl_cons[fb])],
                       lambda e, d=fwl[fb], s=src, t=t, hf=hf: e.dma_start(
                           d, s[t * 128:(t + 1) * 128,
                                hf * 1024:(hf + 1) * 1024]))
            fwl_toks.append(ltok)
            ve([ltok, (T, wst_cons[cb_])],
               lambda e, d=wstream[cb_], s=fwl[fb], hf=hf: e.tensor_copy(
                   d[:, hf * 1024:(hf + 1) * 1024], s), V)
            fwl_cons[fb] = V.n
        cast_v = V.n
        toks = []
        for g in range(4):
            bank = 2 + g % 2
            tb = q_state["tp"] % 3
            q_state["tp"] += 1
            tw = [(V, cast_v)] + (bank_free[bank] or []) + consts_toks
            for i in range(4):
                te(tw if i == 0 else [],
                   lambda e, bk=bank, s=wstream[cb_], g=g, i=i: e.matmul(
                       ps[bk][:, i * 128:(i + 1) * 128],
                       s[:, (g * 4 + i) * 128:(g * 4 + i + 1) * 128],
                       ident, start=True, stop=True),
                   T if i == 3 else None)
            mm = T.n
            se([(T, mm)] + ([q_state["tp_cons"][tb]]
                            if q_state["tp_cons"][tb] else []),
               lambda e, bk=bank, tb=tb: e.copy(tpstage[tb], ps[bk][:]), C)
            bank_free[bank] = [(C, C.n)]
            if w != "o":
                stok = dma("scalar", [(C, C.n)],
                           lambda e, w=w, t=t, g=g, tb=tb: e.dma_start(
                               wT[w][t][:, g * 512:(g + 1) * 512],
                               tpstage[tb]))
            else:
                stok = dma("scalar", [(C, C.n)],
                           lambda e, t=t, g=g, tb=tb: e.dma_start(
                               woT_h[4 * g:4 * g + 4][
                                   :, :, t * 128:(t + 1) * 128].rearrange(
                                   "j p c -> p j c"),
                               tpstage[tb].rearrange("p (j c) -> p j c",
                                                     j=4)))
            q_state["tp_cons"][tb] = stok
            toks.append(stok)
        wst_cons[cb_] = T.n
        wT_gate[w][t] = toks
    wunit_state = {"i": len(units)}

    if NPHASES <= 1:
        ops_end(nc, refs, ops, Q_S, Q_C, Q_G, se, emit)
        return nc

    x_cons = [0, 0]
    ln1b_cons = [None, None]
    ln1T_cons = [0, 0]
    A_half_tok = [[None, None] for _ in range(NCH)]
    last_exp = [0, 0]
    last_rec = [0, 0]
    nmx_v = [0, 0, 0, 0]
    last_aT_mm = [0, 0]
    last_out_mm = [0, 0]
    ln1T_ready = [None, None]
    ln1_store_all = []

    def emit_ln1_chunk(c):
        pos0 = c * CH
        ln1_store_toks = []
        for i in range(4):
            xb = xbuf[i % 2]
            r0 = pos0 + i * 128
            xtok = dma("sync", [(V, x_cons[i % 2])],
                       lambda e, d=xb, r0=r0: e.dma_start(
                           d, x_h[r0:r0 + 128, :]))
            r_done = emit_ln(xb, 2048, trashA, extra_v=[xtok],
                             extra_c=[xtok])
            lb = ln1b[i % 2]
            ve([(C, r_done)] + ([ln1b_cons[i % 2]] if ln1b_cons[i % 2]
                                else []),
               lambda e, s=xb, d=lb: e.tensor_scalar(
                   d, s, ST0["m"], ST0["r"], ALU.subtract, ALU.mult), V)
            x_cons[i % 2] = V.n
            stok = dma("scalar", [(V, V.n)],
                       lambda e, s=lb, r0=r0: e.dma_start(
                           ln1_hbm[r0:r0 + 128, :], s))
            ln1b_cons[i % 2] = stok
            ln1_store_toks.append(stok)
            ln1_store_all.append(stok)
        ln1_gate = latest(ln1_store_toks)
        par = c % 2
        lt = ln1T[par]
        lnT_toks = []
        for Dc in range(16):
            lnT_toks.append(dma(
                "sync", ln1_gate + [(T, ln1T_cons[par])],
                lambda e, d=lt, Dc=Dc, p0=pos0: e.dma_start(
                    d[:, Dc * 512:(Dc + 1) * 512],
                    ln1_hbm[p0:p0 + CH, Dc * 128:(Dc + 1) * 128],
                    transpose=True)))
        ln1T_ready[par] = latest(lnT_toks)

    emit_ln1_chunk(0)
    for c in range(NCH):
        par = c % 2
        lt = ln1T[par]

        for pi, p in enumerate(("q", "k", "v")):
            ename = "scalar"
            for h in range(H):
                gi = pi * H + h
                rbuf = gi % 3
                wtok = dma("sync", [(T, wst_cons[rbuf])] + wT_gate[p][h],
                           lambda e, d=wstream[rbuf], p=p, h=h: e.dma_start(
                               d, wT[p][h]))
                bank = gi % 2
                tw = [wtok] + ln1T_ready[par] + (bank_free[bank] or [])
                for Dc in range(16):
                    te(tw if Dc == 0 else [],
                       lambda e, b=bank, wl=wstream[rbuf], lt=lt, Dc=Dc:
                       e.matmul(ps[b][:], wl[:, Dc * 128:(Dc + 1) * 128],
                                lt[:, Dc * 512:(Dc + 1) * 512],
                                start=(Dc == 0), stop=(Dc == 15)),
                       T if Dc == 15 else None)
                wst_cons[rbuf] = T.n
                mm_done = T.n
                if p == "q":
                    dst = QT.rearrange("p (g s h) -> p g s h", g=64, s=8)[
                        :, :, :, h]
                    srcp = ps[bank][:].rearrange("p (g s) -> p g s", g=64)
                    se([(T, mm_done)],
                       lambda e, d=dst, s=srcp: e.activation(
                           d, s, ACT.Copy, scale=c_qs), C)
                elif p == "k":
                    dst = KT.rearrange("p (g t s) -> p g t s", g=64, t=16)[
                        :, :, h, :]
                    srcp = ps[bank][:].rearrange("p (g s) -> p g s", g=64)
                    se([(T, mm_done)],
                       lambda e, d=dst, s=srcp: e.activation(d, s, ACT.Copy),
                       C)
                else:
                    dst = VT.rearrange("p (g t s) -> p g t s", g=64, t=16)[
                        :, :, h, :]
                    srcp = ps[bank][:].rearrange("p (g s) -> p g s", g=64)
                    se([(T, mm_done)],
                       lambda e, d=dst, s=srcp: e.activation(d, s, ACT.Copy),
                       C)
                bank_free[bank] = [(ECTR[ename], ECTR[ename].n)]
                emit_prep_step()
        ln1T_cons[par] = T.n
        qkv_done = [(V, V.n), (C, C.n)]

        for g4 in range(16):
            bank = 2 + (g4 % 2)
            tw = qkv_done + (bank_free[bank] or []) + consts_toks
            for i in range(4):
                g8 = 4 * g4 + i
                te(tw if i == 0 else [],
                   lambda e, b=bank, i=i, g8=g8: e.matmul(
                       ps[b][:, i * 128:(i + 1) * 128],
                       VT[:, g8 * 128:(g8 + 1) * 128], ident,
                       start=True, stop=True),
                   T if i == 3 else None)
            ve([(T, T.n)], lambda e, b=bank, g4=g4: e.tensor_copy(
                VT[:, g4 * 512:(g4 + 1) * 512], ps[b][:]), V)
            bank_free[bank] = [(V, V.n)]
            emit_prep_step()
        vgrp_done = V.n

        if c + 1 < NCH:
            emit_ln1_chunk(c + 1)

        def a5_stage_a(it):
            for hv in range(2):
                g4 = 2 * it + hv
                sbk = (4, 7)[hv] if it % 2 == 0 else (5, 3)[hv]
                tw = [(V, vgrp_done)] + (bank_free[sbk] or []) + consts_toks
                te(tw, lambda e, b=sbk: e.matmul(
                    ps[b][:], ident, mask4bf, start=True, stop=False), None)
                for i in range(4):
                    g8 = 4 * g4 + i
                    te([], lambda e, b=sbk, i=i, g8=g8: e.matmul(
                        ps[b][:, i * 128:(i + 1) * 128],
                        QT[:, g8 * 128:(g8 + 1) * 128],
                        KT[:, g8 * 128:(g8 + 1) * 128],
                        start=False, stop=(i == 3)),
                       T if i == 3 else None)
                sc_mm = T.n
                nm = nmx4[2 * (it % 2) + hv]
                ve([(T, sc_mm), (C, last_exp[hv])],
                   lambda e, b=sbk, d=nm: e.tensor_reduce(
                       d, ps[b][:].rearrange("p (g cc) -> p g cc", g=4),
                       AX.X, ALU.max, negate=True), V)
                nmx_v[2 * (it % 2) + hv] = V.n

        def a5_stage_b(it):
            nonlocal_A = [0]
            for hv in range(2):
                g4 = 2 * it + hv
                sbk = (4, 7)[hv] if it % 2 == 0 else (5, 3)[hv]
                nm = nmx4[2 * (it % 2) + hv]
                ex = expS[hv]
                su = sums4[hv]
                for g in range(4):
                    se([(V, nmx_v[2 * (it % 2) + hv]), (T, last_aT_mm[hv]),
                        (V, last_rec[hv])] if g == 0 else [],
                       lambda e, b=sbk, g=g, ex=ex, nm=nm, su=su:
                       e.activation(ex[:, g * 128:(g + 1) * 128],
                                    ps[b][:, g * 128:(g + 1) * 128],
                                    ACT.Exp, bias=nm[:, g:g + 1],
                                    accum_out=su[:, g:g + 1]), C)
                last_exp[hv] = C.n
                bank_free[sbk] = [(C, C.n)]
                rc = rec4[hv]
                ve([(C, last_exp[hv])], lambda e, s=su, d=rc: e.reciprocal(
                    d, s), V)
                last_rec[hv] = V.n
                dg = diag4[hv]
                ve([(T, last_aT_mm[hv])],
                   lambda e, d=dg, r=rc: e.tensor_tensor(
                       d.rearrange("p (g cc) -> p g cc", g=4),
                       ident4.rearrange("p (g cc) -> p g cc", g=4),
                       r.broadcast_to([128, 4, 128]), ALU.mult), V)
                diag_done = V.n
                at = attnT_sb[hv]
                tw = [(V, diag_done)] + (bank_free[6] or [])
                for i in range(4):
                    te(tw if i == 0 else [],
                       lambda e, i=i, s=ex, d=dg: e.matmul(
                           ps[6][:, i * 128:(i + 1) * 128],
                           s[:, i * 128:(i + 1) * 128],
                           d[:, i * 128:(i + 1) * 128],
                           start=True, stop=True),
                       T if i == 3 else None)
                last_aT_mm[hv] = T.n
                se([(T, last_aT_mm[hv]), (T, last_out_mm[hv])],
                   lambda e, d=at: e.copy(d, ps[6][:]), C)
                bank_free[6] = [(C, C.n)]
                aT_evac = C.n
                tw = [(C, aT_evac)] + (bank_free[2] or [])
                for i in range(4):
                    g8 = 4 * g4 + i
                    te(tw if i == 0 else [],
                       lambda e, i=i, g8=g8, s=at: e.matmul(
                           ps[2][:, i * 128:(i + 1) * 128],
                           VT[:, g8 * 128:(g8 + 1) * 128],
                           s[:, i * 128:(i + 1) * 128],
                           start=True, stop=True),
                       T if i == 3 else None)
                last_out_mm[hv] = T.n
                war = []
                if c >= 1:
                    war = A_half_tok[c - 1][0 if it < 4 else 1]
                for parx in range(2):
                    srcA = ps[2][:].rearrange("p (i a s h) -> p i a s h",
                                              i=2, a=2, s=8)[:, :, parx, :, :]
                    dstA = Abuf.rearrange("p (a s g h) -> p a s g h",
                                          a=2, s=8, g=32)[
                        :, parx, :, 2 * g4: 2 * g4 + 2, :].rearrange(
                        "p s g h -> p g s h")
                    ve([(T, last_out_mm[hv])] + war,
                       lambda e, s=srcA, d=dstA: e.tensor_copy(d, s), V)
                bank_free[2] = [(V, V.n)]
                nonlocal_A[0] = V.n
            emit_prep_light()
            if it == 3 or it == 7:
                hb = 0 if it == 3 else 1
                toks = []
                for bb in (2 * hb, 2 * hb + 1):
                    toks.append(dma(
                        "scalar", [(V, nonlocal_A[0])],
                        lambda e, c=c, bb=bb: e.dma_start(
                            A_hbm[4 * c + bb].rearrange(
                                "p (j cc) -> p j cc", j=16),
                            Abuf.rearrange("p (j b cc) -> p j b cc",
                                           j=16, b=4)[:, :, bb, :])))
                A_half_tok[c][hb] = latest(toks)

        for it in range(9):
            if it < 8:
                a5_stage_a(it)
            if it >= 1:
                a5_stage_b(it - 1)

    while emit_prep_step():
        pass
    assert g_state["i"] == len(g_tiles) and q_state["i"] == len(q_tiles)
    w1T_gate = latest(q_state["store_toks"][0])
    w2T_gate = latest(q_state["store_toks"][1])
    w3T_gate = latest(q_state["store_toks"][2])
    woT_gate = latest([t for h in range(16) for t in wT_gate["o"][h]])

    if NPHASES <= 2:
        ops_end(nc, refs, ops, Q_S, Q_C, Q_G, se, emit)
        return nc

    hid = BF[:, 640:23168]
    lnT = [BF[:, 23168 + i * 8192: 23168 + (i + 1) * 8192] for i in range(2)]
    wpool = [BF[:, 39552 + i * 2048: 39552 + (i + 1) * 2048]
             for i in range(4)]
    fstage = BF[:, 47744:55936]
    Aslab = [BF[:, 55936 + i * 2048: 55936 + (i + 1) * 2048]
             for i in range(4)]
    Ostb = [BF[:, 64128 + i * 2048: 64128 + (i + 1) * 2048] for i in range(4)]
    ln3b = [BF[:, 72832 + i * 2048: 72832 + (i + 1) * 2048] for i in range(2)]
    wostrip = [BF[:, 76928:77440], BF[:, 77440:77952],
               BF[:, 77952:78464], BF[:, 72320:72832]]
    xg = [xbuf[0], xbuf[1],
          FA[:, 6400:8448], FA[:, 8448:10496]]
    sigp = [FA[:, 4864:5376], FA[:, 5376:5888]]
    trashF = trashA

    snapA = ([(T, T.n), (V, V.n), (C, C.n)]
             + Q_S.tokens() + Q_C.tokens() + Q_G.tokens())

    strip_cons = [0, 0, 0, 0]
    group_mm_done = [0]
    aslab_tok = [None] * 4
    xg_tok = [None] * 4
    xg_cons = [[], [], [], []]
    ostb_cons = [[], [], [], []]
    ostb_evac = [None] * 4
    s1_trash_c = [0]
    x2_store_tok = [None] * 32
    ln3b_cons = [None, None]
    lnT_cons = [0, 0]
    lnT_fill = [[], []]
    pool_cons = [0, 0, 0, 0]
    fst_cons = [0, 0, 0, 0]
    hid_war = [0]
    ln4_prev_v = [0]
    s2_stt = [0] * 4
    s2_ln3c = [0] * 4

    def o_group_loads(g):
        for cbq in range(4):
            cb = 4 * g + cbq
            tk = [(C, s1_trash_c[0])] if cbq == 0 else []
            aslab_tok[cbq] = dma(
                "scalar",
                snapA + tk + A_half_tok[g][0 if cbq < 2 else 1]
                + [(T, group_mm_done[0])],
                lambda e, d=Aslab[cbq], cb=cb: e.dma_start(d, A_hbm[cb]))
        for sl in range(2):
            cb = 4 * g + sl
            xg_tok[sl] = dma(
                "scalar", snapA + xg_cons[sl],
                lambda e, d=xg[sl], cb=cb: e.dma_start(
                    d, x_h.rearrange("(h g) dd -> g h dd", h=16)[
                        8 * cb: 8 * cb + 8]))

    def o_group_loads2(g):
        for sl in (2, 3):
            cb = 4 * g + sl
            xg_tok[sl] = dma(
                "scalar", snapA + xg_cons[sl],
                lambda e, d=xg[sl], cb=cb: e.dma_start(
                    d, x_h.rearrange("(h g) dd -> g h dd", h=16)[
                        8 * cb: 8 * cb + 8]))

    def o_pass(g, dc):
        mm_done = [0] * 4
        for j in range(16):
            sl = j % 4
            eng = "gpsimd" if j % 2 == 0 else "sync"
            stok = dma(eng, snapA + [(T, strip_cons[sl])] + woT_gate,
                       lambda e, d=wostrip[sl], j=j, dc=dc: e.dma_start(
                           d, woT_h[j][:, dc * 512:(dc + 1) * 512]))
            for cbq in range(4):
                tw = []
                if j == 0:
                    tw = [aslab_tok[cbq]] + (bank_free[cbq] or [])
                if cbq == 0:
                    tw = tw + [stok]
                do_inc = (cbq == 3) or (j == 15)
                te(tw, lambda e, cbq=cbq, sl=sl, j=j: e.matmul(
                    ps[cbq][:], Aslab[cbq][:, j * 128:(j + 1) * 128],
                    wostrip[sl], start=(j == 0), stop=(j == 15)),
                   T if do_inc else None)
                if cbq == 3:
                    strip_cons[sl] = T.n
                if j == 15:
                    mm_done[cbq] = T.n
        for cbq in range(4):
            eng = "scalar" if cbq < 2 else "vector"
            ww = [(T, mm_done[cbq])]
            if dc == 0:
                ww = ww + ostb_cons[cbq] + snapA
            EMIT[eng](ww,
                      lambda e, cbq=cbq, dc=dc, eng=eng:
                      (e.copy if eng == "scalar" else e.tensor_copy)(
                          Ostb[cbq][:, dc * 512:(dc + 1) * 512],
                          ps[cbq][:]), ECTR[eng])
            bank_free[cbq] = [(ECTR[eng], ECTR[eng].n)]
            ostb_evac[cbq] = (ECTR[eng], ECTR[eng].n)
        group_mm_done[0] = T.n

    def o_s1(g, cbq):
        v_done, c_done = emit_ln_v(Ostb[cbq], cbq, Aslab[0][:, 0:2048],
                                   extra_v=[ostb_evac[cbq], (C, C.n)],
                                   extra_c=[ostb_evac[cbq], (V, V.n)])
        s1_trash_c[0] = c_done
        return c_done

    def o_s2(g, cbq):
        st = stat(cbq)
        bsl = cbq % 2
        cb = 4 * g + cbq
        ve([(C, C.n)], lambda e, st=st: e.tensor_scalar(
            st["mr"], st["m"], st["r"], None, ALU.mult), V)
        ve([xg_tok[cbq]], lambda e, st=st, cbq=cbq: e.tensor_scalar(
            xg[cbq], xg[cbq], st["mr"], None, ALU.subtract), V)
        ve([], lambda e, st=st, cbq=cbq: e.scalar_tensor_tensor(
            xg[cbq], Ostb[cbq], st["r"], xg[cbq], ALU.mult, ALU.add), V)
        s2_stt[cbq] = V.n
        stok = dma("scalar", [(V, V.n)],
                   lambda e, cbq=cbq, cb=cb: e.dma_start(
                       x2_hbm[cb * 128:(cb + 1) * 128, :], xg[cbq]))
        x2_store_tok[cb] = stok
        v3, c3 = emit_ln_v(xg[cbq], cbq, Aslab[0][:, 0:2048],
                           extra_c=[(V, V.n)])
        s1_trash_c[0] = c3
        s2_ln3c[cbq] = c3
        ve([(C, c3)] + snapA + ([ln3b_cons[bsl]] if ln3b_cons[bsl] else []),
           lambda e, st=st, bsl=bsl, cbq=cbq: e.tensor_scalar(
               ln3b[bsl], xg[cbq], st["m"], st["r"],
               ALU.subtract, ALU.mult), V)
        xg_cons[cbq] = [stok, (V, V.n), (C, c3)]
        ostb_cons[cbq] = [(V, s2_stt[cbq]), (C, c3)]

    def lnT_transpose(g, cbq):
        sl = cbq % 2
        dst = lnT[g % 2]
        if cbq == 0:
            lnT_fill[g % 2] = []
        for eg in range(4):
            bank = eg % 2
            tw = [(V, V.n)] + (bank_free[bank] or []) + consts_toks
            for i in range(4):
                te(tw if i == 0 else [],
                   lambda e, bk=bank, sl=sl, eg=eg, i=i: e.matmul(
                       ps[bk][:, i * 128:(i + 1) * 128],
                       ln3b[sl][:, (eg * 4 + i) * 128:(eg * 4 + i + 1) * 128],
                       ident, start=True, stop=True),
                   T if i == 3 else None)
            mm = T.n
            eng = "vector" if eg % 2 == 0 else "scalar"
            dste = dst.rearrange("p (Dc z) -> p Dc z", Dc=16)[
                :, eg * 4:(eg + 1) * 4, cbq * 128:(cbq + 1) * 128]
            EMIT[eng]([(T, mm), (T, lnT_cons[g % 2])] + snapA,
                      lambda e, d=dste, bk=bank, eng=eng:
                      (e.copy if eng == "scalar" else e.tensor_copy)(
                          d, ps[bk][:].rearrange("p (i z) -> p i z", i=4)),
                      ECTR[eng])
            bank_free[bank] = [(ECTR[eng], ECTR[eng].n)]
            lnT_fill[g % 2].append((ECTR[eng], ECTR[eng].n))
        ln3b_cons[sl] = (T, mm)

    def hidden_section(r, cp_lo, cp_hi):
        for cp in range(cp_lo, cp_hi):
            s1 = cp % 2
            s2 = 2 + cp % 2
            w1_ld = dma("sync", snapA + [(T, pool_cons[s1])] + w1T_gate,
                        lambda e, cp=cp, s1=s1: e.dma_start(
                            wpool[s1], w1T_h[cp]))
            w2_ld = dma("sync", snapA + [(T, pool_cons[s2])] + w2T_gate,
                        lambda e, cp=cp, s2=s2: e.dma_start(
                            wpool[s2], w2T_h[cp]))
            b1 = cp % 2
            b2 = 2 + cp % 2
            lt = lnT[r % 2]
            tw = [w1_ld] + (bank_free[b1] or [])
            if cp == cp_lo:
                tw = tw + latest(lnT_fill[r % 2])
            for Dc in range(16):
                te(tw if Dc == 0 else [],
                   lambda e, b=b1, s1=s1, lt=lt, Dc=Dc: e.matmul(
                       ps[b][:], wpool[s1][:, Dc * 128:(Dc + 1) * 128],
                       lt[:, Dc * 512:(Dc + 1) * 512],
                       start=(Dc == 0), stop=(Dc == 15)),
                   T if Dc == 15 else None)
            x1_mm = T.n
            tw = [w2_ld] + (bank_free[b2] or [])
            for Dc in range(16):
                te(tw if Dc == 0 else [],
                   lambda e, b=b2, s2=s2, lt=lt, Dc=Dc: e.matmul(
                       ps[b][:], wpool[s2][:, Dc * 128:(Dc + 1) * 128],
                       lt[:, Dc * 512:(Dc + 1) * 512],
                       start=(Dc == 0), stop=(Dc == 15)),
                   T if Dc == 15 else None)
            x2_mm = T.n
            pool_cons[s1] = x1_mm
            pool_cons[s2] = x2_mm
            sg = sigp[cp % 2]
            se([(T, x1_mm), (V, sig_war[cp % 2])],
               lambda e, b=b1, sg=sg: e.activation(sg, ps[b][:],
                                                   ACT.Sigmoid), C)
            bank_free[b1] = [(C, C.n)]
            mw = [(C, C.n), (T, x2_mm)]
            if cp == 0:
                mw = mw + [(T, hid_war[0])] + latest(ln1_store_all)
            ve(mw, lambda e, b=b2, cp=cp, sg=sg: e.tensor_tensor(
                hid[:, cp * 512:(cp + 1) * 512], ps[b][:], sg,
                ALU.mult), V)
            nonlocal_sig[0] = V.n
            sig_war[cp % 2] = V.n
            bank_free[b2] = [(V, V.n)]

    nonlocal_sig = [0]
    sig_war = [0, 0]

    def out_sweep(r, dc):
        hid_done = nonlocal_sig[0]
        mm_of = [0] * 4
        for grp in range(11):
            sl = grp % 2
            eng = "scalar" if grp % 2 == 0 else "sync"
            w3tok = dma(eng, snapA + [(T, pool_cons[sl])] + w3T_gate,
                        lambda e, sl=sl, dc=dc, grp=grp: e.dma_start(
                            wpool[sl],
                            w3T_h[dc][:, grp * 4 * 512:(grp + 1) * 4 * 512]))
            for ccl in range(4):
                cc = grp * 4 + ccl
                for posb in range(4):
                    bank = 4 + posb
                    tw = []
                    if cc == 0:
                        tw = [(V, hid_done)] + (bank_free[bank] or [])
                    if ccl == 0 and posb == 0:
                        tw = tw + [w3tok]
                    do_inc = (cc == 43) or (posb == 3 and ccl == 3)
                    te(tw, lambda e, b=bank, sl=sl, ccl=ccl, cc=cc,
                       posb=posb: e.matmul(
                           ps[b][:],
                           hid[:, cc * 512 + posb * 128:
                               cc * 512 + (posb + 1) * 128],
                           wpool[sl][:, ccl * 512:(ccl + 1) * 512],
                           start=(cc == 0), stop=(cc == 43)),
                       T if do_inc else None)
                    if posb == 3 and ccl == 3:
                        pool_cons[sl] = T.n
                    if cc == 43:
                        mm_of[posb] = T.n
        for posb in range(4):
            bank = 4 + posb
            eng = "scalar" if posb % 2 == 0 else "vector"
            fsl = fstage[:, posb * 2048 + dc * 512:
                         posb * 2048 + (dc + 1) * 512]
            EMIT[eng]([(T, mm_of[posb]), (V, fst_cons[posb])] + snapA,
                      lambda e, b=bank, fsl=fsl, eng=eng:
                      (e.copy if eng == "scalar" else e.tensor_copy)(
                          fsl, ps[b][:]), ECTR[eng])
            bank_free[bank] = [(ECTR[eng], ECTR[eng].n)]
            evac_tok = (ECTR[eng], ECTR[eng].n)
            scol = ln4_scol + posb * 4 + dc
            qcol = ln4_qcol + posb * 4 + dc
            ve([evac_tok, (V, ln4_prev_v[0])],
               lambda e, fsl=fsl, scol=scol: e.tensor_reduce(
                   gtmp[:, scol:scol + 1], fsl, AX.X, ALU.add), V)
            se([evac_tok, (V, ln4_prev_v[0])],
               lambda e, fsl=fsl, qcol=qcol: e.activation(
                   trashF, fsl, ACT.Square,
                   accum_out=gtmp[:, qcol:qcol + 1]), C)
        return mm_of[3]

    def ln4_store(r, posb):
        st = stat(posb)
        sl = 2 + posb % 2
        cb = 4 * r + posb
        xtok = dma("gpsimd",
                   snapA + xg_cons[sl] + [x2_store_tok[cb]],
                   lambda e, d=xg[sl], cb=cb: e.dma_start(
                       d, x2_hbm[cb * 128:(cb + 1) * 128, :]))
        ve([(C, C.n)], lambda e, st=st, posb=posb: e.tensor_reduce(
            st["sum"], gtmp[:, ln4_scol + posb * 4: ln4_scol + posb * 4 + 4],
            AX.X, ALU.add), V)
        ve([], lambda e, st=st, posb=posb: e.tensor_reduce(
            st["sq"], gtmp[:, ln4_qcol + posb * 4: ln4_qcol + posb * 4 + 4],
            AX.X, ALU.add), V)
        ve([], lambda e, st=st: e.tensor_scalar(
            st["m"], st["sum"], 1.0 / D, None, ALU.mult), V)
        ve([], lambda e, st=st: e.tensor_scalar(
            st["t1"], st["m"], st["m"], None, ALU.mult), V)
        ve([], lambda e, st=st: e.tensor_scalar(
            st["nv"], st["t1"], -1.0, EPS_LN, ALU.mult, ALU.add), V)
        ve([], lambda e, st=st: e.tensor_scalar(
            st["msq"], st["sq"], 1.0 / D, None, ALU.mult), V)
        ve([], lambda e, st=st: e.tensor_tensor(
            st["t1"], st["msq"], st["nv"], ALU.add), V)
        ve([], lambda e, st=st: e.reciprocal(st["nv"], st["t1"]), V)
        se([(V, V.n)], lambda e, st=st: e.activation(
            st["r"], st["nv"], ACT.Sqrt), C)
        return st, sl, cb, xtok, C.n

    def ln4_finish(posb, st, sl, cb, xtok, r4):
        ve([(C, r4)], lambda e, st=st: e.tensor_scalar(
            st["mr"], st["m"], st["r"], None, ALU.mult), V)
        ve([xtok], lambda e, st=st, sl=sl: e.tensor_scalar(
            xg[sl], xg[sl], st["mr"], None, ALU.subtract), V)
        fsl = fstage[:, posb * 2048:(posb + 1) * 2048]
        ve([], lambda e, st=st, sl=sl, fsl=fsl: e.scalar_tensor_tensor(
            xg[sl], fsl, st["r"], xg[sl], ALU.mult, ALU.add), V)
        fst_cons[posb] = V.n
        ln4_prev_v[0] = V.n
        stok = dma("scalar", [(V, V.n)],
                   lambda e, sl=sl, cb=cb: e.dma_start(
                       out_h.rearrange("(h g) dd -> g h dd", h=16)[
                           8 * cb: 8 * cb + 8], xg[sl]))
        xg_cons[sl] = [stok, (V, V.n)]

    o_group_loads(0)
    o_group_loads2(0)
    for dc in range(4):
        o_pass(0, dc)
    act_load("sqrt_and_friends")
    for cbq in range(4):
        o_s1(0, cbq)
    o_s2(0, 0)
    o_s2(0, 1)
    lnT_transpose(0, 0)
    lnT_transpose(0, 1)
    o_s2(0, 2)
    lnT_transpose(0, 2)
    o_s2(0, 3)
    lnT_transpose(0, 3)
    act_load("sigmoid_and_others")

    if NPHASES <= 3:
        ops_end(nc, refs, ops, Q_S, Q_C, Q_G, se, emit)
        return nc

    for r in range(NRB):
        gn = r + 1
        hidden_section(r, 0, 22)
        if gn < NRB:
            o_group_loads(gn)
            o_group_loads2(gn)
            for dc in range(4):
                o_pass(gn, dc)
        hidden_section(r, 22, 44)
        lnT_cons[r % 2] = T.n
        act_load("sqrt_and_friends")
        if gn < NRB:
            for cbq in range(4):
                o_s1(gn, cbq)
            o_s2(gn, 0)
            o_s2(gn, 1)
        out_sweep(r, 0)
        if gn < NRB:
            lnT_transpose(gn, 0)
            lnT_transpose(gn, 1)
            o_s2(gn, 2)
            o_s2(gn, 3)
        out_sweep(r, 1)
        if gn < NRB:
            lnT_transpose(gn, 2)
            lnT_transpose(gn, 3)
        out_sweep(r, 2)
        hid_war[0] = out_sweep(r, 3)
        ln4p = [ln4_store(r, posb) for posb in range(2)]
        for posb in range(2):
            ln4_finish(posb, *ln4p[posb])
        ln4p2 = [ln4_store(r, posb) for posb in range(2, 4)]
        for posb in range(2, 4):
            ln4_finish(posb, *ln4p2[posb - 2])
        act_load("sigmoid_and_others")

    ops_end(nc, refs, ops, Q_S, Q_C, Q_G, se, emit)
    return nc


def _consts():
    eye = np.eye(128, dtype=ml_dtypes.bfloat16)
    ident4 = np.concatenate([eye] * 4, axis=1)
    p = np.arange(128)[:, None]
    cc = np.arange(128)[None, :]
    m = np.where((cc % 8) == (p // 16), 0.0, NEG).astype(np.float32)
    mask4 = np.concatenate([m] * 4, axis=1)
    return eye, ident4, mask4


def kernel(L0_attn_x, L0_attn_wq, L0_attn_wk, L0_attn_wv, L0_attn_wo,
           L0_ffn_w1, L0_ffn_w2, L0_ffn_w3):
    x = np.ascontiguousarray(np.asarray(L0_attn_x, dtype=np.float32))
    ws = {n: np.ascontiguousarray(np.asarray(a, dtype=np.float32))
          for n, a in (("wq", L0_attn_wq), ("wk", L0_attn_wk),
                       ("wv", L0_attn_wv), ("wo", L0_attn_wo),
                       ("w1", L0_ffn_w1), ("w2", L0_ffn_w2),
                       ("w3", L0_ffn_w3))}
    eye, ident4, mask4 = _consts()
    mask4b = mask4.astype(ml_dtypes.bfloat16)

    nc = build_nc(debug=DEBUG_DUMP)
    in_maps = []
    for b in range(NCORES):
        m = {"x": np.ascontiguousarray(x[b]), "ident": eye,
             "ident4": ident4, "mask4b": mask4b}
        m.update(ws)
        in_maps.append(m)
    res = run_bass_kernel_spmd(nc, in_maps, core_ids=list(range(NCORES)))
    out = np.stack([np.asarray(res.results[b]["out"], dtype=np.float32)
                    for b in range(NCORES)], axis=0)
    if DEBUG_DUMP:
        kernel._debug = res
    return out


# version 16
# speedup vs baseline: 1.0341x; 1.0341x over previous
import contextlib
import numpy as np
import ml_dtypes
import concourse.bass as bass
import concourse.mybir as mybir
from concourse.bass_utils import run_bass_kernel_spmd

F32 = mybir.dt.float32
BF16 = mybir.dt.bfloat16
AX = mybir.AxisListType
ALU = mybir.AluOpType
ACT = mybir.ActivationFunctionType

DEBUG_DUMP = False
NPHASES = 4

S, D, H, HD, HF = 4096, 2048, 16, 128, 5632
CH = 512
NCH = S // CH
RB = 512
NRB = S // RB
NCORES = 8
QSCALE = float(1.0 / np.sqrt(128.0))
EPS_LN = 1e-5
EPS_Q = 1e-8
NW = float(HF * D)
NEG = -1e9


def latest(toks):
    d = {}
    for c, v in toks:
        k = id(c)
        if k not in d or d[k][1] < v:
            d[k] = (c, v)
    return list(d.values())


def ops_end(nc, refs, ops, Q_S, Q_C, Q_G, se, emit):
    se(Q_C.tokens(), lambda e: e.nop(), None)
    emit("sync", Q_S.tokens(), lambda e: e.nop(), None)
    emit("gpsimd", Q_G.tokens(), lambda e: e.nop(), None)
    import contextlib as _ctx
    with _ctx.ExitStack() as stack:
        for name, ref in refs.items():
            ref.sem = stack.enter_context(nc.semaphore(name))
        with nc.Block() as block:
            @block.sync
            def _(e):
                for fn in ops["sync"]:
                    fn(e)

            @block.gpsimd
            def _(e):
                for fn in ops["gpsimd"]:
                    fn(e)

            @block.vector
            def _(e):
                for fn in ops["vector"]:
                    fn(e)

            @block.scalar
            def _(e):
                for fn in ops["scalar"]:
                    fn(e)

            @block.tensor
            def _(e):
                for fn in ops["tensor"]:
                    fn(e)


def build_nc(debug=False):
    nc = bass.Bass("TRN2", target_bir_lowering=False)
    scratch_kind = "ExternalOutput" if debug else "Internal"

    x_h = nc.dram_tensor("x", [S, D], F32, kind="ExternalInput").ap()
    win = {w: nc.dram_tensor(w, [D, D], F32, kind="ExternalInput").ap()
           for w in ("wq", "wk", "wv", "wo")}
    w1_h = nc.dram_tensor("w1", [HF, D], F32, kind="ExternalInput").ap()
    w2_h = nc.dram_tensor("w2", [HF, D], F32, kind="ExternalInput").ap()
    w3_h = nc.dram_tensor("w3", [D, HF], F32, kind="ExternalInput").ap()
    ident_h = nc.dram_tensor("ident", [128, 128], BF16, kind="ExternalInput").ap()
    ident4_h = nc.dram_tensor("ident4", [128, 512], BF16, kind="ExternalInput").ap()
    mask4b_h = nc.dram_tensor("mask4b", [128, 512], BF16,
                              kind="ExternalInput").ap()
    out_h = nc.dram_tensor("out", [S, D], F32, kind="ExternalOutput").ap()

    wT = {w: nc.dram_tensor(f"wT_{w}", [H, 128, D], BF16, kind="Internal").ap()
          for w in ("q", "k", "v")}
    woT_h = nc.dram_tensor("woT", [16, 128, D], BF16, kind="Internal").ap()
    w1T_h = nc.dram_tensor("w1T", [44, 128, 2048], BF16, kind=scratch_kind).ap()
    w2T_h = nc.dram_tensor("w2T", [44, 128, 2048], BF16, kind="Internal").ap()
    w3T_h = nc.dram_tensor("w3T", [4, 128, 44 * 512], BF16,
                           kind=scratch_kind).ap()
    ln1_hbm = nc.dram_tensor("ln1s", [S, D], BF16, kind=scratch_kind).ap()
    A_hbm = nc.dram_tensor("As", [32, 128, 2048], BF16, kind=scratch_kind).ap()
    x2_hbm = nc.dram_tensor("x2s", [S, D], F32, kind=scratch_kind).ap()

    NBF = 78592
    NFA = 10496
    BF = nc.alloc_sbuf_tensor("bfar", [128, NBF], BF16,
                              is_transpose_dest=True).ap()
    FA = nc.alloc_sbuf_tensor("f32ar", [128, NFA], F32).ap()
    assert nc.sbuf_bytes_remaining >= 0

    ident = BF[:, 0:128]
    ident4 = BF[:, 128:640]
    sigb = FA[:, 0:512]
    ones = FA[:, 512:640]
    gtmp = FA[:, 640:768]

    ps = [nc.alloc_psum_tensor(f"ps{i}", [128, 512], F32).ap() for i in range(8)]

    ops = {k: [] for k in ("tensor", "vector", "scalar", "sync", "gpsimd")}
    last_wait = {k: {} for k in ops}

    class SemRef:
        sem = None

    class Ctr:
        def __init__(self, ref, step):
            self.ref, self.n, self.step = ref, 0, step

        @property
        def sem(self):
            return self.ref.sem

        def inc(self):
            self.n += self.step
            return self.n

    NQ = 4
    refs = {}

    def _mkctr(name, step):
        refs[name] = SemRef()
        return Ctr(refs[name], step)

    class Queue:
        def __init__(self, name, n=NQ):
            self.slots = [_mkctr(f"{name}{i}", 16) for i in range(n)]
            self.i = 0

        def next_slot(self):
            s = self.slots[self.i % len(self.slots)]
            self.i += 1
            return s

        def tokens(self):
            return [(s, s.n) for s in self.slots]

    Q_S = Queue("dS")
    Q_C = Queue("dC")
    Q_G = Queue("dG", 4)
    T = _mkctr("T", 1)
    V = _mkctr("V", 1)
    C = _mkctr("C", 1)

    def emit(eng, waits, build, inc=None, fresh=False):
        if eng == "vector":
            waits = list(waits) + [(V, V.n)]
        elif eng == "scalar":
            waits = list(waits) + [(C, C.n)]
        ws = []
        for ctr, val in waits:
            if ctr is None or val <= 0:
                continue
            key = id(ctr)
            if not (fresh and ctr in (T, V, C)):
                if last_wait[eng].get(key, -1) >= val:
                    continue
            if last_wait[eng].get(key, -1) < val:
                last_wait[eng][key] = val
            ws.append((ctr, val))
        if inc is not None:
            inc.inc()

        def step(e, ws=ws, build=build, inc=inc):
            for ctr, val in ws:
                e.wait_ge(ctr.sem, val)
            ins = build(e)
            if inc is not None:
                if ins is None:
                    e.nop().then_inc(inc.sem, inc.step)
                else:
                    ins.then_inc(inc.sem, inc.step)
        ops[eng].append(step)

    def te(waits, build, inc=None):
        emit("tensor", waits, build, inc)

    def ve(waits, build, inc=None):
        emit("vector", waits, build, inc)

    def se(waits, build, inc=None):
        emit("scalar", waits, build, inc)

    EMIT = {"vector": ve, "scalar": se}
    ECTR = {"vector": V, "scalar": C}
    QENG = {"sync": Q_S, "scalar": Q_C, "gpsimd": Q_G}

    def dma(eng, waits, build):
        q = QENG[eng]
        slot = q.next_slot()
        emit(eng, list(waits) + [(slot, slot.n)], build, inc=slot, fresh=True)
        return (slot, slot.n)

    def stat(si):
        b = 8 * si
        return {"sum": gtmp[:, b:b + 1], "sq": gtmp[:, b + 1:b + 2],
                "m": gtmp[:, b + 2:b + 3], "t1": gtmp[:, b + 3:b + 4],
                "nv": gtmp[:, b + 4:b + 5], "r": gtmp[:, b + 5:b + 6],
                "mr": gtmp[:, b + 6:b + 7], "msq": gtmp[:, b + 7:b + 8],
                "sq4": gtmp[:, 40 + 4 * si:44 + 4 * si]}
    ST0 = stat(0)
    nmx4 = [gtmp[:, 8:12], gtmp[:, 56:60], gtmp[:, 20:24], gtmp[:, 24:28]]
    sums4 = [gtmp[:, 12:16], gtmp[:, 60:64]]
    rec4 = [gtmp[:, 16:20], gtmp[:, 64:68]]
    ln4_scol = 60
    ln4_qcol = 76
    trashF4 = gtmp[:, 108:112]
    thrn = gtmp[:, 112:115]
    thr = gtmp[:, 116:119]
    c_1oD = gtmp[:, 122:123]
    c_eps = gtmp[:, 124:125]
    c_nhalf = gtmp[:, 125:126]
    c_none = gtmp[:, 126:127]
    c_qs = gtmp[:, 127:128]

    from concourse.hw_specs import get_activation_tables
    _tables = list(get_activation_tables(nc.m.arch).keys())

    def act_load(set_name):
        idx = _tables.index(set_name)

        def b(e, idx=idx):
            return e.add_instruction(mybir.InstLoadActFuncSet(
                name=nc.get_next_instruction_name(),
                act_func_set_id=idx, ins=[], outs=[]))
        se([], b, None)

    def emit_ln(src, width, trash, extra_v=(), extra_c=()):
        assert width == 2048
        ve(list(extra_v), lambda e: e.tensor_reduce(
            ST0["sum"], src, AX.X, ALU.add), V)
        ve([], lambda e: e.tensor_scalar(ST0["m"], ST0["sum"], 1.0 / width,
                                         None, ALU.mult), V)
        m_v = V.n
        for i in range(4):
            se(list(extra_c) if i == 0 else [],
               lambda e, i=i: e.activation(
                   trash, src[:, i * 512:(i + 1) * 512],
                   ACT.Square, accum_out=ST0["sq4"][:, i:i + 1]), C)
        se([], lambda e: e.activation(
            trashF4, ST0["sq4"], ACT.Copy, accum_out=ST0["sq"]), C)
        se([(V, m_v)], lambda e: e.activation(ST0["msq"], ST0["m"],
                                              ACT.Square), C)
        se([], lambda e: e.activation(ST0["nv"], ST0["msq"], ACT.Copy,
                                      scale=-1.0, bias=EPS_LN), C)
        se([], lambda e: e.activation(
            ST0["t1"], ST0["sq"], ACT.Ln, scale=1.0 / D, bias=ST0["nv"]), C)
        se([], lambda e: e.activation(ST0["r"], ST0["t1"], ACT.Exp,
                                      scale=c_nhalf), C)
        return C.n

    def emit_ln_v(src, si, trash, extra_v=(), extra_c=()):
        st = stat(si)
        se(list(extra_c), lambda e: e.activation(
            trash, src, ACT.Square, accum_out=st["sq"]), C)
        sq_c = C.n
        ve(list(extra_v), lambda e: e.tensor_reduce(
            st["sum"], src, AX.X, ALU.add), V)
        ve([], lambda e: e.tensor_scalar(st["m"], st["sum"], 1.0 / D, None,
                                         ALU.mult), V)
        ve([], lambda e: e.tensor_scalar(st["t1"], st["m"], st["m"], None,
                                         ALU.mult), V)
        ve([], lambda e: e.tensor_scalar(st["nv"], st["t1"], -1.0, EPS_LN,
                                         ALU.mult, ALU.add), V)
        ve([(C, sq_c)], lambda e: e.tensor_scalar(
            st["msq"], st["sq"], 1.0 / D, None, ALU.mult), V)
        ve([], lambda e: e.tensor_tensor(st["t1"], st["msq"], st["nv"],
                                         ALU.add), V)
        ve([], lambda e: e.reciprocal(st["nv"], st["t1"]), V)
        v_done = V.n
        se([(V, v_done)],
           lambda e: e.activation(st["r"], st["nv"], ACT.Sqrt), C)
        return v_done, C.n

    xbuf = [FA[:, 768 + i * 2048: 768 + (i + 1) * 2048] for i in range(2)]
    smask = [FA[:, 4864 + i * 512: 4864 + (i + 1) * 512] for i in range(2)]
    trashA = FA[:, 5888:6400]
    wload_g = [FA[:, 6400 + i * 1024: 6400 + (i + 1) * 1024] for i in range(4)]

    ln1b = [BF[:, 640 + i * 2048: 640 + (i + 1) * 2048] for i in range(2)]
    ln1T = [BF[:, 4736 + i * 8192: 4736 + (i + 1) * 8192] for i in range(2)]
    wstream = [BF[:, 21120 + i * 2048: 21120 + (i + 1) * 2048]
               for i in range(3)]
    QT = BF[:, 27264:35456]
    KT = BF[:, 35456:43648]
    VT = BF[:, 43648:51840]
    Abuf = BF[:, 51840:60032]
    expS = [BF[:, 60032 + i * 512: 60032 + (i + 1) * 512] for i in range(2)]
    attnT_sb = [BF[:, 61056 + i * 512: 61056 + (i + 1) * 512]
                for i in range(2)]
    diag4 = [BF[:, 62080 + i * 512: 62080 + (i + 1) * 512] for i in range(2)]
    mask4bf = BF[:, 63104:63616]
    wload_q = [BF[:, 63616 + i * 2048: 63616 + (i + 1) * 2048].bitcast(F32)
               for i in range(2)]
    lbuf = BF[:, 67712:69760].bitcast(F32)
    qstage = [BF[:, 69760 + i * 1024: 69760 + (i + 1) * 1024]
              for i in range(2)]
    tpstage = [BF[:, 71808:72320], BF[:, 72320:72832], BF[:, 77952:78464]]

    consts_toks = [
        dma("sync", [], lambda e: e.dma_start(ident, ident_h)),
        dma("sync", [], lambda e: e.dma_start(ident4, ident4_h)),
        dma("sync", [], lambda e: e.dma_start(mask4bf, mask4b_h)),
    ]
    act_load("natural_log_exp_and_others")
    ve([], lambda e: e.memset(ones, 1.0), V)
    ve([], lambda e: e.memset(c_eps, EPS_LN), V)
    ve([], lambda e: e.memset(c_nhalf, -0.5), V)
    ve([], lambda e: e.memset(c_none, -1.0), V)
    ve([], lambda e: e.memset(c_qs, QSCALE), V)
    ve([], lambda e: e.memset(c_1oD, 1.0 / D), V)
    nc.const_aps.register(F32, EPS_LN, c_eps)
    nc.const_aps.register(F32, -1.0, c_none)
    nc.const_aps.register(F32, 1.0 / D, c_1oD)
    ones_done = V.n

    bank_free = {i: None for i in range(8)}

    g_tiles = []
    for wi, (srcw, nt, wdt) in enumerate(((w1_h, 44, 2048), (w2_h, 44, 2048),
                                          (w3_h, 16, 5632))):
        for t in range(nt):
            for c0 in range(0, wdt, 1024):
                cw = min(1024, wdt - c0)
                g_tiles.append((wi, srcw, t, c0, cw))
    g_last = {}
    for i, g in enumerate(g_tiles):
        g_last[g[0]] = i
    g_state = {"i": 0, "cons": [0, 0, 0, 0],
               "first": {0: True, 1: True, 2: True}, "done_v": {}}
    thr_state = {"done": {0: False, 1: False, 2: False}, "v": {}}

    def emit_gamma_step():
        if g_state["i"] >= len(g_tiles):
            return False
        wi, srcw, t, c0, cw = g_tiles[g_state["i"]]
        b = g_state["i"] % 4
        idx = g_state["i"]
        g_state["i"] += 1
        eng = "gpsimd" if idx % 2 == 0 else "scalar"
        tok = dma(eng, [(V, g_state["cons"][b])],
                  lambda e, d=wload_g[b], s=srcw, t=t, c0=c0, cw=cw:
                  e.dma_start(d[:, 0:cw],
                              s[t * 128:(t + 1) * 128, c0:c0 + cw]))
        acol = 96 + wi
        if g_state["first"][wi]:
            g_state["first"][wi] = False
            ve([tok], lambda e, s=wload_g[b], cw=cw, a=acol: e.tensor_reduce(
                gtmp[:, a:a + 1], s[:, 0:cw], AX.X, ALU.add,
                apply_absolute_value=True), V)
        else:
            ve([tok], lambda e, s=wload_g[b], cw=cw, a=acol: e.tensor_reduce(
                gtmp[:, a + 8:a + 9], s[:, 0:cw], AX.X, ALU.add,
                apply_absolute_value=True), V)
            ve([], lambda e, a=acol: e.tensor_tensor(
                gtmp[:, a:a + 1], gtmp[:, a:a + 1], gtmp[:, a + 8:a + 9],
                ALU.add), V)
        g_state["cons"][b] = V.n
        if idx == g_last[wi]:
            g_state["done_v"][wi] = V.n
        return True

    def emit_thr_finalize(wi):
        acol = 96 + wi
        te([(V, g_state["done_v"][wi]), (V, ones_done)]
           + (bank_free[4] or []),
           lambda e, a=acol: e.matmul(ps[4][0:1, a - 96:a - 95], ones[:, 0:1],
                                      gtmp[:, a:a + 1],
                                      start=True, stop=True), T)
        thr_row = gtmp[0:1, 92 + wi:93 + wi]
        ve([(T, T.n)],
           lambda e, d=thr_row, wi=wi: e.tensor_scalar(
               d, ps[4][0:1, wi:wi + 1], 0.5 / NW, 0.5 * EPS_Q,
               ALU.mult, ALU.add), V)
        te([(V, V.n)],
           lambda e, s=thr_row, wi=wi: e.matmul(
               ps[4][0:128, 8 + wi:9 + wi], ones[0:1, 0:128], s,
               start=True, stop=True), T)
        ve([(T, T.n)], lambda e, wi=wi: e.tensor_copy(
            thr[:, wi:wi + 1], ps[4][:, 8 + wi:9 + wi]), V)
        ve([], lambda e, wi=wi: e.tensor_scalar(
            thrn[:, wi:wi + 1], thr[:, wi:wi + 1], -1.0, None, ALU.mult), V)
        thr_state["v"][wi] = V.n
        bank_free[4] = [(V, V.n)]

    q_tiles = []
    for wi, (srcw, nt, wdt) in enumerate(((w1_h, 44, 2048), (w2_h, 44, 2048),
                                          (w3_h, 16, 5632))):
        for t in range(nt):
            for c0 in range(0, wdt, 1024):
                cw = min(1024, wdt - c0)
                q_tiles.append((wi, srcw, t, c0, cw))
    q_state = {"i": 0, "cons": [0, 0, 0, 0, 0, 0], "tp": 0,
               "tp_cons": [None, None, None],
               "mm": [0, 0, 0, 0, 0, 0], "first_gate": [],
               "store_toks": {0: [], 1: [], 2: []}}

    def emit_quant_step():
        if q_state["i"] >= len(q_tiles):
            return False
        wi, srcw, t, c0, cw = q_tiles[q_state["i"]]
        if not thr_state["done"][wi]:
            return False
        if q_state["i"] == 0:
            assert wunit_state["i"] == len(units)
            assert g_state["i"] == len(g_tiles)
            fcmax = max(fwl_cons)
            q_state["cons"] = [fcmax, fcmax] + list(g_state["cons"])
            q_state["first_gate"] = latest(fwl_toks)
        b = q_state["i"] % 6
        q_state["i"] += 1
        wld = (wload_q + wload_g)[b]
        tok = dma("gpsimd", [(V, q_state["cons"][b])],
                  lambda e, d=wld, s=srcw, t=t, c0=c0, cw=cw:
                  e.dma_start(d[:, 0:cw],
                              s[t * 128:(t + 1) * 128, c0:c0 + cw]))
        fg = q_state["first_gate"]
        q_state["first_gate"] = []
        qb = q_state["i"] % 2
        ve([tok, (V, thr_state["v"][wi])] + fg,
           lambda e, s=wld, cw=cw, wi=wi: e.tensor_scalar(
               lbuf[:, 0:cw], s[:, 0:cw], thrn[:, wi:wi + 1], None,
               ALU.is_le), V)
        ve([(T, q_state["mm"][qb])],
           lambda e, s=wld, q=qstage[qb], cw=cw, wi=wi:
           e.scalar_tensor_tensor(q[:, 0:cw], s[:, 0:cw],
                                  thr[:, wi:wi + 1], lbuf[:, 0:cw],
                                  ALU.is_ge, ALU.subtract), V)
        q_state["cons"][b] = V.n
        stt_v = V.n
        for g in range(cw // 512):
            bank = 2 + q_state["tp"] % 2
            tb = q_state["tp"] % 3
            q_state["tp"] += 1
            tw = [(V, stt_v)] + (bank_free[bank] or []) + consts_toks
            for i in range(4):
                te(tw if i == 0 else [],
                   lambda e, bk=bank, q=qstage[qb], g=g, i=i: e.matmul(
                       ps[bk][:, i * 128:(i + 1) * 128],
                       q[:, g * 512 + i * 128: g * 512 + (i + 1) * 128],
                       ident, start=True, stop=True),
                   T if i == 3 else None)
            mm = T.n
            q_state["mm"][qb] = mm
            se([(T, mm)] + ([q_state["tp_cons"][tb]]
                            if q_state["tp_cons"][tb] else []),
               lambda e, bk=bank, tb=tb: e.copy(tpstage[tb], ps[bk][:]), C)
            bank_free[bank] = [(C, C.n)]
            if wi < 2:
                wTd = w1T_h if wi == 0 else w2T_h
                stok = dma("scalar", [(C, C.n)],
                           lambda e, d=wTd, t=t, c0=c0, g=g, tb=tb:
                           e.dma_start(
                               d[t][:, c0 + g * 512: c0 + (g + 1) * 512],
                               tpstage[tb]))
            else:
                ccb = (c0 + g * 512) // 128
                zoff = (t % 4) * 128
                stok = dma(
                    "scalar", [(C, C.n)],
                    lambda e, t=t, ccb=ccb, zoff=zoff, tb=tb: e.dma_start(
                        w3T_h[t // 4].rearrange("p (cc z) -> p cc z", z=512)[
                            :, ccb:ccb + 4, zoff:zoff + 128],
                        tpstage[tb].rearrange("p (i z) -> p i z", i=4)))
            q_state["tp_cons"][tb] = stok
            q_state["store_toks"][wi].append(stok)
        return True

    def emit_prep_light():
        if emit_gamma_step():
            emit_gamma_step()
            return True
        return False

    def emit_prep_step():
        if emit_prep_light():
            return True
        for wi in range(3):
            if (wi in g_state["done_v"] and not thr_state["done"][wi]):
                emit_thr_finalize(wi)
                thr_state["done"][wi] = True
                return True
        return emit_quant_step()

    fwl = [BF[:, 63616 + i * 2048: 63616 + (i + 1) * 2048].bitcast(F32)
           for i in range(4)]
    fwl_cons = [0, 0, 0, 0]
    fwl_toks = []
    wst_cons = [0, 0, 0]
    wT_gate = {w: {} for w in ("q", "k", "v", "o")}
    units = [(w, t) for w in ("q", "k", "v", "o") for t in range(16)]
    for u, (w, t) in enumerate(units):
        cb_ = u % 3
        src = win["w" + w]
        for hf in range(2):
            fb = (2 * u + hf) % 4
            ltok = dma("sync" if hf == 0 else "gpsimd",
                       [(V, fwl_cons[fb])],
                       lambda e, d=fwl[fb], s=src, t=t, hf=hf: e.dma_start(
                           d, s[t * 128:(t + 1) * 128,
                                hf * 1024:(hf + 1) * 1024]))
            fwl_toks.append(ltok)
            ve([ltok, (T, wst_cons[cb_])],
               lambda e, d=wstream[cb_], s=fwl[fb], hf=hf: e.tensor_copy(
                   d[:, hf * 1024:(hf + 1) * 1024], s), V)
            fwl_cons[fb] = V.n
        cast_v = V.n
        toks = []
        for g in range(4):
            bank = 2 + g % 2
            tb = q_state["tp"] % 3
            q_state["tp"] += 1
            tw = [(V, cast_v)] + (bank_free[bank] or []) + consts_toks
            for i in range(4):
                te(tw if i == 0 else [],
                   lambda e, bk=bank, s=wstream[cb_], g=g, i=i: e.matmul(
                       ps[bk][:, i * 128:(i + 1) * 128],
                       s[:, (g * 4 + i) * 128:(g * 4 + i + 1) * 128],
                       ident, start=True, stop=True),
                   T if i == 3 else None)
            mm = T.n
            se([(T, mm)] + ([q_state["tp_cons"][tb]]
                            if q_state["tp_cons"][tb] else []),
               lambda e, bk=bank, tb=tb: e.copy(tpstage[tb], ps[bk][:]), C)
            bank_free[bank] = [(C, C.n)]
            if w != "o":
                stok = dma("scalar", [(C, C.n)],
                           lambda e, w=w, t=t, g=g, tb=tb: e.dma_start(
                               wT[w][t][:, g * 512:(g + 1) * 512],
                               tpstage[tb]))
            else:
                stok = dma("scalar", [(C, C.n)],
                           lambda e, t=t, g=g, tb=tb: e.dma_start(
                               woT_h[4 * g:4 * g + 4][
                                   :, :, t * 128:(t + 1) * 128].rearrange(
                                   "j p c -> p j c"),
                               tpstage[tb].rearrange("p (j c) -> p j c",
                                                     j=4)))
            q_state["tp_cons"][tb] = stok
            toks.append(stok)
        wst_cons[cb_] = T.n
        wT_gate[w][t] = toks
    wunit_state = {"i": len(units)}

    if NPHASES <= 1:
        ops_end(nc, refs, ops, Q_S, Q_C, Q_G, se, emit)
        return nc

    x_cons = [0, 0]
    ln1b_cons = [None, None]
    ln1T_cons = [0, 0]
    A_half_tok = [[None, None] for _ in range(NCH)]
    last_exp = [0, 0]
    last_rec = [0, 0]
    nmx_v = [0, 0, 0, 0]
    last_aT_mm = [0, 0]
    last_out_mm = [0, 0]
    ln1T_ready = [None, None]
    ln1_store_all = []

    def emit_ln1_chunk(c):
        pos0 = c * CH
        ln1_store_toks = []
        for i in range(4):
            xb = xbuf[i % 2]
            r0 = pos0 + i * 128
            xtok = dma("sync", [(V, x_cons[i % 2])],
                       lambda e, d=xb, r0=r0: e.dma_start(
                           d, x_h[r0:r0 + 128, :]))
            r_done = emit_ln(xb, 2048, trashA, extra_v=[xtok],
                             extra_c=[xtok])
            lb = ln1b[i % 2]
            ve([(C, r_done)] + ([ln1b_cons[i % 2]] if ln1b_cons[i % 2]
                                else []),
               lambda e, s=xb, d=lb: e.tensor_scalar(
                   d, s, ST0["m"], ST0["r"], ALU.subtract, ALU.mult), V)
            x_cons[i % 2] = V.n
            stok = dma("scalar", [(V, V.n)],
                       lambda e, s=lb, r0=r0: e.dma_start(
                           ln1_hbm[r0:r0 + 128, :], s))
            ln1b_cons[i % 2] = stok
            ln1_store_toks.append(stok)
            ln1_store_all.append(stok)
        ln1_gate = latest(ln1_store_toks)
        par = c % 2
        lt = ln1T[par]
        lnT_toks = []
        for Dc in range(16):
            lnT_toks.append(dma(
                "sync", ln1_gate + [(T, ln1T_cons[par])],
                lambda e, d=lt, Dc=Dc, p0=pos0: e.dma_start(
                    d[:, Dc * 512:(Dc + 1) * 512],
                    ln1_hbm[p0:p0 + CH, Dc * 128:(Dc + 1) * 128],
                    transpose=True)))
        ln1T_ready[par] = latest(lnT_toks)

    emit_ln1_chunk(0)
    for c in range(NCH):
        par = c % 2
        lt = ln1T[par]

        for pi, p in enumerate(("q", "k", "v")):
            ename = "vector" if p == "v" else "scalar"
            for h in range(H):
                gi = pi * H + h
                rbuf = gi % 3
                wtok = dma("sync", [(T, wst_cons[rbuf])] + wT_gate[p][h],
                           lambda e, d=wstream[rbuf], p=p, h=h: e.dma_start(
                               d, wT[p][h]))
                bank = gi % 2
                tw = [wtok] + ln1T_ready[par] + (bank_free[bank] or [])
                for Dc in range(16):
                    te(tw if Dc == 0 else [],
                       lambda e, b=bank, wl=wstream[rbuf], lt=lt, Dc=Dc:
                       e.matmul(ps[b][:], wl[:, Dc * 128:(Dc + 1) * 128],
                                lt[:, Dc * 512:(Dc + 1) * 512],
                                start=(Dc == 0), stop=(Dc == 15)),
                       T if Dc == 15 else None)
                wst_cons[rbuf] = T.n
                mm_done = T.n
                if p == "q":
                    dst = QT.rearrange("p (g s h) -> p g s h", g=64, s=8)[
                        :, :, :, h]
                    srcp = ps[bank][:].rearrange("p (g s) -> p g s", g=64)
                    se([(T, mm_done)],
                       lambda e, d=dst, s=srcp: e.activation(
                           d, s, ACT.Copy, scale=c_qs), C)
                elif p == "k":
                    dst = KT.rearrange("p (g t s) -> p g t s", g=64, t=16)[
                        :, :, h, :]
                    srcp = ps[bank][:].rearrange("p (g s) -> p g s", g=64)
                    se([(T, mm_done)],
                       lambda e, d=dst, s=srcp: e.activation(d, s, ACT.Copy),
                       C)
                else:
                    dst = VT.rearrange("p (g t s) -> p g t s", g=64, t=16)[
                        :, :, h, :]
                    srcp = ps[bank][:].rearrange("p (g s) -> p g s", g=64)
                    ve([(T, mm_done)],
                       lambda e, d=dst, s=srcp: e.tensor_copy(d, s), V)
                bank_free[bank] = [(ECTR[ename], ECTR[ename].n)]
                emit_prep_step()
        ln1T_cons[par] = T.n
        qkv_done = [(V, V.n), (C, C.n)]

        for g4 in range(16):
            bank = 2 + (g4 % 2)
            tw = qkv_done + (bank_free[bank] or []) + consts_toks
            for i in range(4):
                g8 = 4 * g4 + i
                te(tw if i == 0 else [],
                   lambda e, b=bank, i=i, g8=g8: e.matmul(
                       ps[b][:, i * 128:(i + 1) * 128],
                       VT[:, g8 * 128:(g8 + 1) * 128], ident,
                       start=True, stop=True),
                   T if i == 3 else None)
            ve([(T, T.n)], lambda e, b=bank, g4=g4: e.tensor_copy(
                VT[:, g4 * 512:(g4 + 1) * 512], ps[b][:]), V)
            bank_free[bank] = [(V, V.n)]
            emit_prep_step()
        vgrp_done = V.n

        if c + 1 < NCH:
            emit_ln1_chunk(c + 1)

        def a5_stage_a(it):
            for hv in range(2):
                g4 = 2 * it + hv
                sbk = (4, 7)[hv] if it % 2 == 0 else (5, 3)[hv]
                tw = [(V, vgrp_done)] + (bank_free[sbk] or []) + consts_toks
                te(tw, lambda e, b=sbk: e.matmul(
                    ps[b][:], ident, mask4bf, start=True, stop=False), None)
                for i in range(4):
                    g8 = 4 * g4 + i
                    te([], lambda e, b=sbk, i=i, g8=g8: e.matmul(
                        ps[b][:, i * 128:(i + 1) * 128],
                        QT[:, g8 * 128:(g8 + 1) * 128],
                        KT[:, g8 * 128:(g8 + 1) * 128],
                        start=False, stop=(i == 3)),
                       T if i == 3 else None)
                sc_mm = T.n
                nm = nmx4[2 * (it % 2) + hv]
                ve([(T, sc_mm), (C, last_exp[hv])],
                   lambda e, b=sbk, d=nm: e.tensor_reduce(
                       d, ps[b][:].rearrange("p (g cc) -> p g cc", g=4),
                       AX.X, ALU.max, negate=True), V)
                nmx_v[2 * (it % 2) + hv] = V.n

        def a5_stage_b(it):
            nonlocal_A = [0]
            for hv in range(2):
                g4 = 2 * it + hv
                sbk = (4, 7)[hv] if it % 2 == 0 else (5, 3)[hv]
                nm = nmx4[2 * (it % 2) + hv]
                ex = expS[hv]
                su = sums4[hv]
                for g in range(4):
                    se([(V, nmx_v[2 * (it % 2) + hv]), (T, last_aT_mm[hv]),
                        (V, last_rec[hv])] if g == 0 else [],
                       lambda e, b=sbk, g=g, ex=ex, nm=nm, su=su:
                       e.activation(ex[:, g * 128:(g + 1) * 128],
                                    ps[b][:, g * 128:(g + 1) * 128],
                                    ACT.Exp, bias=nm[:, g:g + 1],
                                    accum_out=su[:, g:g + 1]), C)
                last_exp[hv] = C.n
                bank_free[sbk] = [(C, C.n)]
                rc = rec4[hv]
                ve([(C, last_exp[hv])], lambda e, s=su, d=rc: e.reciprocal(
                    d, s), V)
                last_rec[hv] = V.n
                dg = diag4[hv]
                ve([(T, last_aT_mm[hv])],
                   lambda e, d=dg, r=rc: e.tensor_tensor(
                       d.rearrange("p (g cc) -> p g cc", g=4),
                       ident4.rearrange("p (g cc) -> p g cc", g=4),
                       r.broadcast_to([128, 4, 128]), ALU.mult), V)
                diag_done = V.n
                at = attnT_sb[hv]
                tw = [(V, diag_done)] + (bank_free[6] or [])
                for i in range(4):
                    te(tw if i == 0 else [],
                       lambda e, i=i, s=ex, d=dg: e.matmul(
                           ps[6][:, i * 128:(i + 1) * 128],
                           s[:, i * 128:(i + 1) * 128],
                           d[:, i * 128:(i + 1) * 128],
                           start=True, stop=True),
                       T if i == 3 else None)
                last_aT_mm[hv] = T.n
                se([(T, last_aT_mm[hv]), (T, last_out_mm[hv])],
                   lambda e, d=at: e.copy(d, ps[6][:]), C)
                bank_free[6] = [(C, C.n)]
                aT_evac = C.n
                tw = [(C, aT_evac)] + (bank_free[2] or [])
                for i in range(4):
                    g8 = 4 * g4 + i
                    te(tw if i == 0 else [],
                       lambda e, i=i, g8=g8, s=at: e.matmul(
                           ps[2][:, i * 128:(i + 1) * 128],
                           VT[:, g8 * 128:(g8 + 1) * 128],
                           s[:, i * 128:(i + 1) * 128],
                           start=True, stop=True),
                       T if i == 3 else None)
                last_out_mm[hv] = T.n
                war = []
                if c >= 1:
                    war = A_half_tok[c - 1][0 if it < 4 else 1]
                for parx in range(2):
                    srcA = ps[2][:].rearrange("p (i a s h) -> p i a s h",
                                              i=2, a=2, s=8)[:, :, parx, :, :]
                    dstA = Abuf.rearrange("p (a s g h) -> p a s g h",
                                          a=2, s=8, g=32)[
                        :, parx, :, 2 * g4: 2 * g4 + 2, :].rearrange(
                        "p s g h -> p g s h")
                    ve([(T, last_out_mm[hv])] + war,
                       lambda e, s=srcA, d=dstA: e.tensor_copy(d, s), V)
                bank_free[2] = [(V, V.n)]
                nonlocal_A[0] = V.n
            emit_prep_light()
            if it == 3 or it == 7:
                hb = 0 if it == 3 else 1
                toks = []
                for bb in (2 * hb, 2 * hb + 1):
                    toks.append(dma(
                        "scalar", [(V, nonlocal_A[0])],
                        lambda e, c=c, bb=bb: e.dma_start(
                            A_hbm[4 * c + bb].rearrange(
                                "p (j cc) -> p j cc", j=16),
                            Abuf.rearrange("p (j b cc) -> p j b cc",
                                           j=16, b=4)[:, :, bb, :])))
                A_half_tok[c][hb] = latest(toks)

        for it in range(9):
            if it < 8:
                a5_stage_a(it)
            if it >= 1:
                a5_stage_b(it - 1)

    while emit_prep_step():
        pass
    assert g_state["i"] == len(g_tiles) and q_state["i"] == len(q_tiles)
    w1T_gate = latest(q_state["store_toks"][0])
    w2T_gate = latest(q_state["store_toks"][1])
    w3T_gate = latest(q_state["store_toks"][2])
    woT_gate = latest([t for h in range(16) for t in wT_gate["o"][h]])

    if NPHASES <= 2:
        ops_end(nc, refs, ops, Q_S, Q_C, Q_G, se, emit)
        return nc

    hid = BF[:, 640:23168]
    lnT = [BF[:, 23168 + i * 8192: 23168 + (i + 1) * 8192] for i in range(2)]
    wpool = [BF[:, 39552 + i * 2048: 39552 + (i + 1) * 2048]
             for i in range(4)]
    fstage = BF[:, 47744:55936]
    Aslab = [BF[:, 55936 + i * 2048: 55936 + (i + 1) * 2048]
             for i in range(4)]
    Ostb = [BF[:, 64128 + i * 2048: 64128 + (i + 1) * 2048] for i in range(4)]
    ln3b = [BF[:, 72832 + i * 2048: 72832 + (i + 1) * 2048] for i in range(2)]
    wostrip = [BF[:, 76928:77440], BF[:, 77440:77952],
               BF[:, 77952:78464], BF[:, 72320:72832]]
    xg = [xbuf[0], xbuf[1],
          FA[:, 6400:8448], FA[:, 8448:10496]]
    sigp = [FA[:, 4864:5376], FA[:, 5376:5888]]
    trashF = trashA

    snapA = ([(T, T.n), (V, V.n), (C, C.n)]
             + Q_S.tokens() + Q_C.tokens() + Q_G.tokens())

    strip_cons = [0, 0, 0, 0]
    group_mm_done = [0]
    aslab_tok = [None] * 4
    xg_tok = [None] * 4
    xg_cons = [[], [], [], []]
    ostb_cons = [[], [], [], []]
    ostb_evac = [None] * 4
    s1_trash_c = [0]
    x2_store_tok = [None] * 32
    ln3b_cons = [None, None]
    lnT_cons = [0, 0]
    lnT_fill = [[], []]
    pool_cons = [0, 0, 0, 0]
    fst_cons = [0, 0, 0, 0]
    hid_war = [0]
    ln4_prev_v = [0]
    s2_stt = [0] * 4
    s2_ln3c = [0] * 4

    def o_group_loads(g):
        for cbq in range(4):
            cb = 4 * g + cbq
            tk = [(C, s1_trash_c[0])] if cbq == 0 else []
            aslab_tok[cbq] = dma(
                "scalar",
                snapA + tk + A_half_tok[g][0 if cbq < 2 else 1]
                + [(T, group_mm_done[0])],
                lambda e, d=Aslab[cbq], cb=cb: e.dma_start(d, A_hbm[cb]))
        for sl in range(2):
            cb = 4 * g + sl
            xg_tok[sl] = dma(
                "scalar", snapA + xg_cons[sl],
                lambda e, d=xg[sl], cb=cb: e.dma_start(
                    d, x_h.rearrange("(h g) dd -> g h dd", h=16)[
                        8 * cb: 8 * cb + 8]))

    def o_group_loads2(g):
        for sl in (2, 3):
            cb = 4 * g + sl
            xg_tok[sl] = dma(
                "scalar", snapA + xg_cons[sl],
                lambda e, d=xg[sl], cb=cb: e.dma_start(
                    d, x_h.rearrange("(h g) dd -> g h dd", h=16)[
                        8 * cb: 8 * cb + 8]))

    def o_pass(g, dc):
        mm_done = [0] * 4
        for j in range(16):
            sl = j % 4
            eng = "gpsimd" if j % 2 == 0 else "sync"
            stok = dma(eng, snapA + [(T, strip_cons[sl])] + woT_gate,
                       lambda e, d=wostrip[sl], j=j, dc=dc: e.dma_start(
                           d, woT_h[j][:, dc * 512:(dc + 1) * 512]))
            for cbq in range(4):
                tw = []
                if j == 0:
                    tw = [aslab_tok[cbq]] + (bank_free[cbq] or [])
                if cbq == 0:
                    tw = tw + [stok]
                do_inc = (cbq == 3) or (j == 15)
                te(tw, lambda e, cbq=cbq, sl=sl, j=j: e.matmul(
                    ps[cbq][:], Aslab[cbq][:, j * 128:(j + 1) * 128],
                    wostrip[sl], start=(j == 0), stop=(j == 15)),
                   T if do_inc else None)
                if cbq == 3:
                    strip_cons[sl] = T.n
                if j == 15:
                    mm_done[cbq] = T.n
        for cbq in range(4):
            eng = "scalar" if cbq < 2 else "vector"
            ww = [(T, mm_done[cbq])]
            if dc == 0:
                ww = ww + ostb_cons[cbq] + snapA
            EMIT[eng](ww,
                      lambda e, cbq=cbq, dc=dc, eng=eng:
                      (e.copy if eng == "scalar" else e.tensor_copy)(
                          Ostb[cbq][:, dc * 512:(dc + 1) * 512],
                          ps[cbq][:]), ECTR[eng])
            bank_free[cbq] = [(ECTR[eng], ECTR[eng].n)]
            ostb_evac[cbq] = (ECTR[eng], ECTR[eng].n)
        group_mm_done[0] = T.n

    def o_s1(g, cbq):
        v_done, c_done = emit_ln_v(Ostb[cbq], cbq, Aslab[0][:, 0:2048],
                                   extra_v=[ostb_evac[cbq], (C, C.n)],
                                   extra_c=[ostb_evac[cbq], (V, V.n)])
        s1_trash_c[0] = c_done
        return c_done

    def o_s2(g, cbq):
        st = stat(cbq)
        bsl = cbq % 2
        cb = 4 * g + cbq
        ve([(C, C.n)], lambda e, st=st: e.tensor_scalar(
            st["mr"], st["m"], st["r"], None, ALU.mult), V)
        ve([xg_tok[cbq]], lambda e, st=st, cbq=cbq: e.tensor_scalar(
            xg[cbq], xg[cbq], st["mr"], None, ALU.subtract), V)
        ve([], lambda e, st=st, cbq=cbq: e.scalar_tensor_tensor(
            xg[cbq], Ostb[cbq], st["r"], xg[cbq], ALU.mult, ALU.add), V)
        s2_stt[cbq] = V.n
        stok = dma("scalar", [(V, V.n)],
                   lambda e, cbq=cbq, cb=cb: e.dma_start(
                       x2_hbm[cb * 128:(cb + 1) * 128, :], xg[cbq]))
        x2_store_tok[cb] = stok
        v3, c3 = emit_ln_v(xg[cbq], cbq, Aslab[0][:, 0:2048],
                           extra_c=[(V, V.n)])
        s1_trash_c[0] = c3
        s2_ln3c[cbq] = c3
        ve([(C, c3)] + snapA + ([ln3b_cons[bsl]] if ln3b_cons[bsl] else []),
           lambda e, st=st, bsl=bsl, cbq=cbq: e.tensor_scalar(
               ln3b[bsl], xg[cbq], st["m"], st["r"],
               ALU.subtract, ALU.mult), V)
        xg_cons[cbq] = [stok, (V, V.n), (C, c3)]
        ostb_cons[cbq] = [(V, s2_stt[cbq]), (C, c3)]

    def lnT_transpose(g, cbq):
        sl = cbq % 2
        dst = lnT[g % 2]
        if cbq == 0:
            lnT_fill[g % 2] = []
        for eg in range(4):
            bank = eg % 2
            tw = [(V, V.n)] + (bank_free[bank] or []) + consts_toks
            for i in range(4):
                te(tw if i == 0 else [],
                   lambda e, bk=bank, sl=sl, eg=eg, i=i: e.matmul(
                       ps[bk][:, i * 128:(i + 1) * 128],
                       ln3b[sl][:, (eg * 4 + i) * 128:(eg * 4 + i + 1) * 128],
                       ident, start=True, stop=True),
                   T if i == 3 else None)
            mm = T.n
            eng = "vector" if eg % 2 == 0 else "scalar"
            dste = dst.rearrange("p (Dc z) -> p Dc z", Dc=16)[
                :, eg * 4:(eg + 1) * 4, cbq * 128:(cbq + 1) * 128]
            EMIT[eng]([(T, mm), (T, lnT_cons[g % 2])] + snapA,
                      lambda e, d=dste, bk=bank, eng=eng:
                      (e.copy if eng == "scalar" else e.tensor_copy)(
                          d, ps[bk][:].rearrange("p (i z) -> p i z", i=4)),
                      ECTR[eng])
            bank_free[bank] = [(ECTR[eng], ECTR[eng].n)]
            lnT_fill[g % 2].append((ECTR[eng], ECTR[eng].n))
        ln3b_cons[sl] = (T, mm)

    def hidden_section(r, cp_lo, cp_hi):
        for cp in range(cp_lo, cp_hi):
            s1 = cp % 2
            s2 = 2 + cp % 2
            w1_ld = dma("sync", snapA + [(T, pool_cons[s1])] + w1T_gate,
                        lambda e, cp=cp, s1=s1: e.dma_start(
                            wpool[s1], w1T_h[cp]))
            w2_ld = dma("sync", snapA + [(T, pool_cons[s2])] + w2T_gate,
                        lambda e, cp=cp, s2=s2: e.dma_start(
                            wpool[s2], w2T_h[cp]))
            b1 = cp % 2
            b2 = 2 + cp % 2
            lt = lnT[r % 2]
            tw = [w1_ld] + (bank_free[b1] or [])
            if cp == cp_lo:
                tw = tw + latest(lnT_fill[r % 2])
            for Dc in range(16):
                te(tw if Dc == 0 else [],
                   lambda e, b=b1, s1=s1, lt=lt, Dc=Dc: e.matmul(
                       ps[b][:], wpool[s1][:, Dc * 128:(Dc + 1) * 128],
                       lt[:, Dc * 512:(Dc + 1) * 512],
                       start=(Dc == 0), stop=(Dc == 15)),
                   T if Dc == 15 else None)
            x1_mm = T.n
            tw = [w2_ld] + (bank_free[b2] or [])
            for Dc in range(16):
                te(tw if Dc == 0 else [],
                   lambda e, b=b2, s2=s2, lt=lt, Dc=Dc: e.matmul(
                       ps[b][:], wpool[s2][:, Dc * 128:(Dc + 1) * 128],
                       lt[:, Dc * 512:(Dc + 1) * 512],
                       start=(Dc == 0), stop=(Dc == 15)),
                   T if Dc == 15 else None)
            x2_mm = T.n
            pool_cons[s1] = x1_mm
            pool_cons[s2] = x2_mm
            sg = sigp[cp % 2]
            se([(T, x1_mm), (V, sig_war[cp % 2])],
               lambda e, b=b1, sg=sg: e.activation(sg, ps[b][:],
                                                   ACT.Sigmoid), C)
            bank_free[b1] = [(C, C.n)]
            mw = [(C, C.n), (T, x2_mm)]
            if cp == 0:
                mw = mw + [(T, hid_war[0])] + latest(ln1_store_all)
            ve(mw, lambda e, b=b2, cp=cp, sg=sg: e.tensor_tensor(
                hid[:, cp * 512:(cp + 1) * 512], ps[b][:], sg,
                ALU.mult), V)
            nonlocal_sig[0] = V.n
            sig_war[cp % 2] = V.n
            bank_free[b2] = [(V, V.n)]

    nonlocal_sig = [0]
    sig_war = [0, 0]

    def out_sweep(r, dc):
        hid_done = nonlocal_sig[0]
        mm_of = [0] * 4
        for grp in range(11):
            sl = grp % 4
            eng = "scalar" if grp % 2 == 0 else "sync"
            w3tok = dma(eng, snapA + [(T, pool_cons[sl])] + w3T_gate,
                        lambda e, sl=sl, dc=dc, grp=grp: e.dma_start(
                            wpool[sl],
                            w3T_h[dc][:, grp * 4 * 512:(grp + 1) * 4 * 512]))
            for ccl in range(4):
                cc = grp * 4 + ccl
                for posb in range(4):
                    bank = 4 + posb
                    tw = []
                    if cc == 0:
                        tw = [(V, hid_done)] + (bank_free[bank] or [])
                    if ccl == 0 and posb == 0:
                        tw = tw + [w3tok]
                    do_inc = (cc == 43) or (posb == 3 and ccl == 3)
                    te(tw, lambda e, b=bank, sl=sl, ccl=ccl, cc=cc,
                       posb=posb: e.matmul(
                           ps[b][:],
                           hid[:, cc * 512 + posb * 128:
                               cc * 512 + (posb + 1) * 128],
                           wpool[sl][:, ccl * 512:(ccl + 1) * 512],
                           start=(cc == 0), stop=(cc == 43)),
                       T if do_inc else None)
                    if posb == 3 and ccl == 3:
                        pool_cons[sl] = T.n
                    if cc == 43:
                        mm_of[posb] = T.n
        for posb in range(4):
            bank = 4 + posb
            eng = "scalar" if posb % 2 == 0 else "vector"
            fsl = fstage[:, posb * 2048 + dc * 512:
                         posb * 2048 + (dc + 1) * 512]
            EMIT[eng]([(T, mm_of[posb]), (V, fst_cons[posb])] + snapA,
                      lambda e, b=bank, fsl=fsl, eng=eng:
                      (e.copy if eng == "scalar" else e.tensor_copy)(
                          fsl, ps[b][:]), ECTR[eng])
            bank_free[bank] = [(ECTR[eng], ECTR[eng].n)]
            evac_tok = (ECTR[eng], ECTR[eng].n)
            scol = ln4_scol + posb * 4 + dc
            qcol = ln4_qcol + posb * 4 + dc
            ve([evac_tok, (V, ln4_prev_v[0])],
               lambda e, fsl=fsl, scol=scol: e.tensor_reduce(
                   gtmp[:, scol:scol + 1], fsl, AX.X, ALU.add), V)
            se([evac_tok, (V, ln4_prev_v[0])],
               lambda e, fsl=fsl, qcol=qcol: e.activation(
                   trashF, fsl, ACT.Square,
                   accum_out=gtmp[:, qcol:qcol + 1]), C)
        return mm_of[3]

    def ln4_store(r, posb):
        sl = 2 + posb % 2
        cb = 4 * r + posb
        xtok = dma("gpsimd",
                   snapA + xg_cons[sl] + [x2_store_tok[cb]],
                   lambda e, d=xg[sl], cb=cb: e.dma_start(
                       d, x2_hbm[cb * 128:(cb + 1) * 128, :]))
        ve([(C, C.n)], lambda e, posb=posb: e.tensor_reduce(
            ST0["sum"], gtmp[:, ln4_scol + posb * 4: ln4_scol + posb * 4 + 4],
            AX.X, ALU.add), V)
        ve([], lambda e, posb=posb: e.tensor_reduce(
            ST0["sq"], gtmp[:, ln4_qcol + posb * 4: ln4_qcol + posb * 4 + 4],
            AX.X, ALU.add), V)
        ve([], lambda e: e.tensor_scalar(ST0["m"], ST0["sum"], 1.0 / D, None,
                                         ALU.mult), V)
        ve([], lambda e: e.tensor_scalar(ST0["t1"], ST0["m"], ST0["m"], None,
                                         ALU.mult), V)
        ve([], lambda e: e.tensor_scalar(ST0["nv"], ST0["t1"], -1.0, EPS_LN,
                                         ALU.mult, ALU.add), V)
        ve([], lambda e: e.tensor_scalar(ST0["msq"], ST0["sq"], 1.0 / D, None,
                                         ALU.mult), V)
        ve([], lambda e: e.tensor_tensor(ST0["t1"], ST0["msq"], ST0["nv"],
                                         ALU.add), V)
        ve([], lambda e: e.reciprocal(ST0["nv"], ST0["t1"]), V)
        se([(V, V.n)], lambda e: e.activation(
            ST0["r"], ST0["nv"], ACT.Sqrt), C)
        r4 = C.n
        ve([(C, r4)], lambda e: e.tensor_scalar(
            ST0["mr"], ST0["m"], ST0["r"], None, ALU.mult), V)
        ve([xtok], lambda e, sl=sl: e.tensor_scalar(
            xg[sl], xg[sl], ST0["mr"], None, ALU.subtract), V)
        fsl = fstage[:, posb * 2048:(posb + 1) * 2048]
        ve([], lambda e, sl=sl, fsl=fsl: e.scalar_tensor_tensor(
            xg[sl], fsl, ST0["r"], xg[sl], ALU.mult, ALU.add), V)
        fst_cons[posb] = V.n
        ln4_prev_v[0] = V.n
        stok = dma("scalar", [(V, V.n)],
                   lambda e, sl=sl, cb=cb: e.dma_start(
                       out_h.rearrange("(h g) dd -> g h dd", h=16)[
                           8 * cb: 8 * cb + 8], xg[sl]))
        xg_cons[sl] = [stok, (V, V.n)]

    o_group_loads(0)
    o_group_loads2(0)
    for dc in range(4):
        o_pass(0, dc)
    act_load("sqrt_and_friends")
    for cbq in range(4):
        o_s1(0, cbq)
    o_s2(0, 0)
    o_s2(0, 1)
    lnT_transpose(0, 0)
    lnT_transpose(0, 1)
    o_s2(0, 2)
    lnT_transpose(0, 2)
    o_s2(0, 3)
    lnT_transpose(0, 3)
    act_load("sigmoid_and_others")

    if NPHASES <= 3:
        ops_end(nc, refs, ops, Q_S, Q_C, Q_G, se, emit)
        return nc

    for r in range(NRB):
        gn = r + 1
        hidden_section(r, 0, 22)
        if gn < NRB:
            o_group_loads(gn)
            o_group_loads2(gn)
            for dc in range(4):
                o_pass(gn, dc)
        hidden_section(r, 22, 44)
        lnT_cons[r % 2] = T.n
        act_load("sqrt_and_friends")
        if gn < NRB:
            for cbq in range(4):
                o_s1(gn, cbq)
            o_s2(gn, 0)
            o_s2(gn, 1)
        out_sweep(r, 0)
        if gn < NRB:
            lnT_transpose(gn, 0)
            lnT_transpose(gn, 1)
            o_s2(gn, 2)
            o_s2(gn, 3)
        out_sweep(r, 1)
        if gn < NRB:
            lnT_transpose(gn, 2)
            lnT_transpose(gn, 3)
        out_sweep(r, 2)
        hid_war[0] = out_sweep(r, 3)
        for posb in range(4):
            ln4_store(r, posb)
        act_load("sigmoid_and_others")

    ops_end(nc, refs, ops, Q_S, Q_C, Q_G, se, emit)
    return nc


def _consts():
    eye = np.eye(128, dtype=ml_dtypes.bfloat16)
    ident4 = np.concatenate([eye] * 4, axis=1)
    p = np.arange(128)[:, None]
    cc = np.arange(128)[None, :]
    m = np.where((cc % 8) == (p // 16), 0.0, NEG).astype(np.float32)
    mask4 = np.concatenate([m] * 4, axis=1)
    return eye, ident4, mask4


def kernel(L0_attn_x, L0_attn_wq, L0_attn_wk, L0_attn_wv, L0_attn_wo,
           L0_ffn_w1, L0_ffn_w2, L0_ffn_w3):
    x = np.ascontiguousarray(np.asarray(L0_attn_x, dtype=np.float32))
    ws = {n: np.ascontiguousarray(np.asarray(a, dtype=np.float32))
          for n, a in (("wq", L0_attn_wq), ("wk", L0_attn_wk),
                       ("wv", L0_attn_wv), ("wo", L0_attn_wo),
                       ("w1", L0_ffn_w1), ("w2", L0_ffn_w2),
                       ("w3", L0_ffn_w3))}
    eye, ident4, mask4 = _consts()
    mask4b = mask4.astype(ml_dtypes.bfloat16)

    nc = build_nc(debug=DEBUG_DUMP)
    in_maps = []
    for b in range(NCORES):
        m = {"x": np.ascontiguousarray(x[b]), "ident": eye,
             "ident4": ident4, "mask4b": mask4b}
        m.update(ws)
        in_maps.append(m)
    res = run_bass_kernel_spmd(nc, in_maps, core_ids=list(range(NCORES)))
    out = np.stack([np.asarray(res.results[b]["out"], dtype=np.float32)
                    for b in range(NCORES)], axis=0)
    if DEBUG_DUMP:
        kernel._debug = res
    return out


# version 17
# speedup vs baseline: 1.0685x; 1.0333x over previous
import contextlib
import numpy as np
import ml_dtypes
import concourse.bass as bass
import concourse.mybir as mybir
from concourse.bass_utils import run_bass_kernel_spmd

F32 = mybir.dt.float32
BF16 = mybir.dt.bfloat16
AX = mybir.AxisListType
ALU = mybir.AluOpType
ACT = mybir.ActivationFunctionType

DEBUG_DUMP = False
NPHASES = 4

S, D, H, HD, HF = 4096, 2048, 16, 128, 5632
CH = 512
NCH = S // CH
RB = 512
NRB = S // RB
NCORES = 8
QSCALE = float(1.0 / np.sqrt(128.0))
EPS_LN = 1e-5
EPS_Q = 1e-8
NW = float(HF * D)
NEG = -1e9


def latest(toks):
    d = {}
    for c, v in toks:
        k = id(c)
        if k not in d or d[k][1] < v:
            d[k] = (c, v)
    return list(d.values())


def ops_end(nc, refs, ops, Q_S, Q_C, Q_G, se, emit):
    se(Q_C.tokens(), lambda e: e.nop(), None)
    emit("sync", Q_S.tokens(), lambda e: e.nop(), None)
    emit("gpsimd", Q_G.tokens(), lambda e: e.nop(), None)
    import contextlib as _ctx
    with _ctx.ExitStack() as stack:
        for name, ref in refs.items():
            ref.sem = stack.enter_context(nc.semaphore(name))
        with nc.Block() as block:
            @block.sync
            def _(e):
                for fn in ops["sync"]:
                    fn(e)

            @block.gpsimd
            def _(e):
                for fn in ops["gpsimd"]:
                    fn(e)

            @block.vector
            def _(e):
                for fn in ops["vector"]:
                    fn(e)

            @block.scalar
            def _(e):
                for fn in ops["scalar"]:
                    fn(e)

            @block.tensor
            def _(e):
                for fn in ops["tensor"]:
                    fn(e)


def build_nc(debug=False):
    nc = bass.Bass("TRN2", target_bir_lowering=False)
    scratch_kind = "ExternalOutput" if debug else "Internal"

    x_h = nc.dram_tensor("x", [S, D], F32, kind="ExternalInput").ap()
    win = {w: nc.dram_tensor(w, [D, D], F32, kind="ExternalInput").ap()
           for w in ("wq", "wk", "wv", "wo")}
    w1_h = nc.dram_tensor("w1", [HF, D], F32, kind="ExternalInput").ap()
    w2_h = nc.dram_tensor("w2", [HF, D], F32, kind="ExternalInput").ap()
    w3_h = nc.dram_tensor("w3", [D, HF], F32, kind="ExternalInput").ap()
    ident_h = nc.dram_tensor("ident", [128, 128], BF16, kind="ExternalInput").ap()
    ident4_h = nc.dram_tensor("ident4", [128, 512], BF16, kind="ExternalInput").ap()
    mask4b_h = nc.dram_tensor("mask4b", [128, 512], BF16,
                              kind="ExternalInput").ap()
    out_h = nc.dram_tensor("out", [S, D], F32, kind="ExternalOutput").ap()

    wT = {w: nc.dram_tensor(f"wT_{w}", [H, 128, D], BF16, kind="Internal").ap()
          for w in ("q", "k", "v")}
    woT_h = nc.dram_tensor("woT", [16, 128, D], BF16, kind="Internal").ap()
    w1T_h = nc.dram_tensor("w1T", [44, 128, 2048], BF16, kind=scratch_kind).ap()
    w2T_h = nc.dram_tensor("w2T", [44, 128, 2048], BF16, kind="Internal").ap()
    w3T_h = nc.dram_tensor("w3T", [4, 128, 44 * 512], BF16,
                           kind=scratch_kind).ap()
    ln1_hbm = nc.dram_tensor("ln1s", [S, D], BF16, kind=scratch_kind).ap()
    A_hbm = nc.dram_tensor("As", [32, 128, 2048], BF16, kind=scratch_kind).ap()
    x2_hbm = nc.dram_tensor("x2s", [S, D], F32, kind=scratch_kind).ap()

    NBF = 78592
    NFA = 10496
    BF = nc.alloc_sbuf_tensor("bfar", [128, NBF], BF16,
                              is_transpose_dest=True).ap()
    FA = nc.alloc_sbuf_tensor("f32ar", [128, NFA], F32).ap()
    assert nc.sbuf_bytes_remaining >= 0

    ident = BF[:, 0:128]
    ident4 = BF[:, 128:640]
    sigb = FA[:, 0:512]
    ones = FA[:, 512:640]
    gtmp = FA[:, 640:768]

    ps = [nc.alloc_psum_tensor(f"ps{i}", [128, 512], F32).ap() for i in range(8)]

    ops = {k: [] for k in ("tensor", "vector", "scalar", "sync", "gpsimd")}
    last_wait = {k: {} for k in ops}

    class SemRef:
        sem = None

    class Ctr:
        def __init__(self, ref, step):
            self.ref, self.n, self.step = ref, 0, step

        @property
        def sem(self):
            return self.ref.sem

        def inc(self):
            self.n += self.step
            return self.n

    NQ = 4
    refs = {}

    def _mkctr(name, step):
        refs[name] = SemRef()
        return Ctr(refs[name], step)

    class Queue:
        def __init__(self, name, n=NQ):
            self.slots = [_mkctr(f"{name}{i}", 16) for i in range(n)]
            self.i = 0

        def next_slot(self):
            s = self.slots[self.i % len(self.slots)]
            self.i += 1
            return s

        def tokens(self):
            return [(s, s.n) for s in self.slots]

    Q_S = Queue("dS")
    Q_C = Queue("dC")
    Q_G = Queue("dG", 4)
    T = _mkctr("T", 1)
    V = _mkctr("V", 1)
    C = _mkctr("C", 1)

    def emit(eng, waits, build, inc=None, fresh=False):
        if eng == "vector":
            waits = list(waits) + [(V, V.n)]
        elif eng == "scalar":
            waits = list(waits) + [(C, C.n)]
        ws = []
        for ctr, val in waits:
            if ctr is None or val <= 0:
                continue
            key = id(ctr)
            if not (fresh and ctr in (T, V, C)):
                if last_wait[eng].get(key, -1) >= val:
                    continue
            if last_wait[eng].get(key, -1) < val:
                last_wait[eng][key] = val
            ws.append((ctr, val))
        if inc is not None:
            inc.inc()

        def step(e, ws=ws, build=build, inc=inc):
            for ctr, val in ws:
                e.wait_ge(ctr.sem, val)
            ins = build(e)
            if inc is not None:
                if ins is None:
                    e.nop().then_inc(inc.sem, inc.step)
                else:
                    ins.then_inc(inc.sem, inc.step)
        ops[eng].append(step)

    def te(waits, build, inc=None):
        emit("tensor", waits, build, inc)

    def ve(waits, build, inc=None):
        emit("vector", waits, build, inc)

    def se(waits, build, inc=None):
        emit("scalar", waits, build, inc)

    EMIT = {"vector": ve, "scalar": se}
    ECTR = {"vector": V, "scalar": C}
    QENG = {"sync": Q_S, "scalar": Q_C, "gpsimd": Q_G}

    def dma(eng, waits, build):
        q = QENG[eng]
        slot = q.next_slot()
        emit(eng, list(waits) + [(slot, slot.n)], build, inc=slot, fresh=True)
        return (slot, slot.n)

    def stat(si):
        b = 8 * si
        return {"sum": gtmp[:, b:b + 1], "sq": gtmp[:, b + 1:b + 2],
                "m": gtmp[:, b + 2:b + 3], "t1": gtmp[:, b + 3:b + 4],
                "nv": gtmp[:, b + 4:b + 5], "r": gtmp[:, b + 5:b + 6],
                "mr": gtmp[:, b + 6:b + 7], "msq": gtmp[:, b + 7:b + 8],
                "sq4": gtmp[:, 40 + 4 * si:44 + 4 * si]}
    ST0 = stat(0)
    nmx4 = [gtmp[:, 8:12], gtmp[:, 56:60], gtmp[:, 20:24], gtmp[:, 24:28]]
    sums4 = [gtmp[:, 12:16], gtmp[:, 60:64]]
    rec4 = [gtmp[:, 16:20], gtmp[:, 64:68]]
    ln4_scol = 60
    ln4_qcol = 76
    trashF4 = gtmp[:, 108:112]
    thrn = gtmp[:, 112:115]
    thr = gtmp[:, 116:119]
    c_1oD = gtmp[:, 122:123]
    c_eps = gtmp[:, 124:125]
    c_nhalf = gtmp[:, 125:126]
    c_none = gtmp[:, 126:127]
    c_qs = gtmp[:, 127:128]

    from concourse.hw_specs import get_activation_tables
    _tables = list(get_activation_tables(nc.m.arch).keys())

    def act_load(set_name):
        idx = _tables.index(set_name)

        def b(e, idx=idx):
            return e.add_instruction(mybir.InstLoadActFuncSet(
                name=nc.get_next_instruction_name(),
                act_func_set_id=idx, ins=[], outs=[]))
        se([], b, None)

    def emit_ln(src, width, trash, extra_v=(), extra_c=()):
        assert width == 2048
        ve(list(extra_v), lambda e: e.tensor_reduce(
            ST0["sum"], src, AX.X, ALU.add), V)
        ve([], lambda e: e.tensor_scalar(ST0["m"], ST0["sum"], 1.0 / width,
                                         None, ALU.mult), V)
        m_v = V.n
        for i in range(4):
            se(list(extra_c) if i == 0 else [],
               lambda e, i=i: e.activation(
                   trash, src[:, i * 512:(i + 1) * 512],
                   ACT.Square, accum_out=ST0["sq4"][:, i:i + 1]), C)
        se([], lambda e: e.activation(
            trashF4, ST0["sq4"], ACT.Copy, accum_out=ST0["sq"]), C)
        se([(V, m_v)], lambda e: e.activation(ST0["msq"], ST0["m"],
                                              ACT.Square), C)
        se([], lambda e: e.activation(ST0["nv"], ST0["msq"], ACT.Copy,
                                      scale=-1.0, bias=EPS_LN), C)
        se([], lambda e: e.activation(
            ST0["t1"], ST0["sq"], ACT.Ln, scale=1.0 / D, bias=ST0["nv"]), C)
        se([], lambda e: e.activation(ST0["r"], ST0["t1"], ACT.Exp,
                                      scale=c_nhalf), C)
        return C.n

    def emit_ln_v(src, si, trash, extra_v=(), extra_c=()):
        st = stat(si)
        se(list(extra_c), lambda e: e.activation(
            trash, src, ACT.Square, accum_out=st["sq"]), C)
        sq_c = C.n
        ve(list(extra_v), lambda e: e.tensor_reduce(
            st["sum"], src, AX.X, ALU.add), V)
        ve([], lambda e: e.tensor_scalar(st["m"], st["sum"], 1.0 / D, None,
                                         ALU.mult), V)
        ve([], lambda e: e.tensor_scalar(st["t1"], st["m"], st["m"], None,
                                         ALU.mult), V)
        ve([], lambda e: e.tensor_scalar(st["nv"], st["t1"], -1.0, EPS_LN,
                                         ALU.mult, ALU.add), V)
        ve([(C, sq_c)], lambda e: e.tensor_scalar(
            st["msq"], st["sq"], 1.0 / D, None, ALU.mult), V)
        ve([], lambda e: e.tensor_tensor(st["t1"], st["msq"], st["nv"],
                                         ALU.add), V)
        ve([], lambda e: e.reciprocal(st["nv"], st["t1"]), V)
        v_done = V.n
        se([(V, v_done)],
           lambda e: e.activation(st["r"], st["nv"], ACT.Sqrt), C)
        return v_done, C.n

    xbuf = [FA[:, 768 + i * 2048: 768 + (i + 1) * 2048] for i in range(2)]
    smask = [FA[:, 4864 + i * 512: 4864 + (i + 1) * 512] for i in range(2)]
    trashA = FA[:, 5888:6400]
    wload_g = [FA[:, 6400 + i * 1024: 6400 + (i + 1) * 1024] for i in range(4)]

    ln1b = [BF[:, 640 + i * 2048: 640 + (i + 1) * 2048] for i in range(2)]
    ln1T = [BF[:, 4736 + i * 8192: 4736 + (i + 1) * 8192] for i in range(2)]
    wstream = [BF[:, 21120 + i * 2048: 21120 + (i + 1) * 2048]
               for i in range(3)] + [BF[:, 72832:74880]]
    QT = BF[:, 27264:35456]
    KT = BF[:, 35456:43648]
    VT = BF[:, 43648:51840]
    Abuf = BF[:, 51840:60032]
    expS = [BF[:, 60032 + i * 512: 60032 + (i + 1) * 512] for i in range(2)]
    attnT_sb = [BF[:, 61056 + i * 512: 61056 + (i + 1) * 512]
                for i in range(2)]
    diag4 = [BF[:, 62080 + i * 512: 62080 + (i + 1) * 512] for i in range(2)]
    mask4bf = BF[:, 63104:63616]
    wload_q = [BF[:, 63616 + i * 2048: 63616 + (i + 1) * 2048].bitcast(F32)
               for i in range(2)]
    lbuf = BF[:, 67712:69760].bitcast(F32)
    qstage = [BF[:, 69760 + i * 1024: 69760 + (i + 1) * 1024]
              for i in range(2)]
    tpstage = [BF[:, 71808:72320], BF[:, 72320:72832], BF[:, 77952:78464]]

    consts_toks = [
        dma("sync", [], lambda e: e.dma_start(ident, ident_h)),
        dma("sync", [], lambda e: e.dma_start(ident4, ident4_h)),
        dma("sync", [], lambda e: e.dma_start(mask4bf, mask4b_h)),
    ]
    act_load("natural_log_exp_and_others")
    ve([], lambda e: e.memset(ones, 1.0), V)
    ve([], lambda e: e.memset(c_eps, EPS_LN), V)
    ve([], lambda e: e.memset(c_nhalf, -0.5), V)
    ve([], lambda e: e.memset(c_none, -1.0), V)
    ve([], lambda e: e.memset(c_qs, QSCALE), V)
    ve([], lambda e: e.memset(c_1oD, 1.0 / D), V)
    nc.const_aps.register(F32, EPS_LN, c_eps)
    nc.const_aps.register(F32, -1.0, c_none)
    nc.const_aps.register(F32, 1.0 / D, c_1oD)
    ones_done = V.n

    bank_free = {i: None for i in range(8)}

    g_tiles = []
    for wi, (srcw, nt, wdt) in enumerate(((w1_h, 44, 2048), (w2_h, 44, 2048),
                                          (w3_h, 16, 5632))):
        for t in range(nt):
            for c0 in range(0, wdt, 1024):
                cw = min(1024, wdt - c0)
                g_tiles.append((wi, srcw, t, c0, cw))
    g_last = {}
    for i, g in enumerate(g_tiles):
        g_last[g[0]] = i
    g_state = {"i": 0, "cons": [0, 0, 0, 0],
               "first": {0: True, 1: True, 2: True}, "done_v": {}}
    thr_state = {"done": {0: False, 1: False, 2: False}, "v": {}}

    def emit_gamma_step():
        if g_state["i"] >= len(g_tiles):
            return False
        wi, srcw, t, c0, cw = g_tiles[g_state["i"]]
        b = g_state["i"] % 4
        idx = g_state["i"]
        g_state["i"] += 1
        eng = "gpsimd" if idx % 2 == 0 else "scalar"
        tok = dma(eng, [(V, g_state["cons"][b])],
                  lambda e, d=wload_g[b], s=srcw, t=t, c0=c0, cw=cw:
                  e.dma_start(d[:, 0:cw],
                              s[t * 128:(t + 1) * 128, c0:c0 + cw]))
        acol = 96 + wi
        if g_state["first"][wi]:
            g_state["first"][wi] = False
            ve([tok], lambda e, s=wload_g[b], cw=cw, a=acol: e.tensor_reduce(
                gtmp[:, a:a + 1], s[:, 0:cw], AX.X, ALU.add,
                apply_absolute_value=True), V)
        else:
            ve([tok], lambda e, s=wload_g[b], cw=cw, a=acol: e.tensor_reduce(
                gtmp[:, a + 8:a + 9], s[:, 0:cw], AX.X, ALU.add,
                apply_absolute_value=True), V)
            ve([], lambda e, a=acol: e.tensor_tensor(
                gtmp[:, a:a + 1], gtmp[:, a:a + 1], gtmp[:, a + 8:a + 9],
                ALU.add), V)
        g_state["cons"][b] = V.n
        if idx == g_last[wi]:
            g_state["done_v"][wi] = V.n
        return True

    def emit_thr_finalize(wi):
        acol = 96 + wi
        te([(V, g_state["done_v"][wi]), (V, ones_done)]
           + (bank_free[4] or []),
           lambda e, a=acol: e.matmul(ps[4][0:1, a - 96:a - 95], ones[:, 0:1],
                                      gtmp[:, a:a + 1],
                                      start=True, stop=True), T)
        thr_row = gtmp[0:1, 92 + wi:93 + wi]
        ve([(T, T.n)],
           lambda e, d=thr_row, wi=wi: e.tensor_scalar(
               d, ps[4][0:1, wi:wi + 1], 0.5 / NW, 0.5 * EPS_Q,
               ALU.mult, ALU.add), V)
        te([(V, V.n)],
           lambda e, s=thr_row, wi=wi: e.matmul(
               ps[4][0:128, 8 + wi:9 + wi], ones[0:1, 0:128], s,
               start=True, stop=True), T)
        ve([(T, T.n)], lambda e, wi=wi: e.tensor_copy(
            thr[:, wi:wi + 1], ps[4][:, 8 + wi:9 + wi]), V)
        ve([], lambda e, wi=wi: e.tensor_scalar(
            thrn[:, wi:wi + 1], thr[:, wi:wi + 1], -1.0, None, ALU.mult), V)
        thr_state["v"][wi] = V.n
        bank_free[4] = [(V, V.n)]

    q_tiles = []
    for wi, (srcw, nt, wdt) in enumerate(((w1_h, 44, 2048), (w2_h, 44, 2048),
                                          (w3_h, 16, 5632))):
        for t in range(nt):
            for c0 in range(0, wdt, 1024):
                cw = min(1024, wdt - c0)
                q_tiles.append((wi, srcw, t, c0, cw))
    q_state = {"i": 0, "cons": [0, 0, 0, 0, 0, 0], "tp": 0,
               "tp_cons": [None, None, None],
               "mm": [0, 0, 0, 0, 0, 0], "first_gate": [],
               "store_toks": {0: [], 1: [], 2: []}}

    def emit_quant_step():
        if q_state["i"] >= len(q_tiles):
            return False
        wi, srcw, t, c0, cw = q_tiles[q_state["i"]]
        if not thr_state["done"][wi]:
            return False
        if q_state["i"] == 0:
            assert wunit_state["i"] == len(units)
            assert g_state["i"] == len(g_tiles)
            fcmax = max(fwl_cons)
            q_state["cons"] = [fcmax, fcmax] + list(g_state["cons"])
            q_state["first_gate"] = latest(fwl_toks)
        b = q_state["i"] % 6
        q_state["i"] += 1
        wld = (wload_q + wload_g)[b]
        tok = dma("gpsimd", [(V, q_state["cons"][b])],
                  lambda e, d=wld, s=srcw, t=t, c0=c0, cw=cw:
                  e.dma_start(d[:, 0:cw],
                              s[t * 128:(t + 1) * 128, c0:c0 + cw]))
        fg = q_state["first_gate"]
        q_state["first_gate"] = []
        qb = q_state["i"] % 2
        ve([tok, (V, thr_state["v"][wi])] + fg,
           lambda e, s=wld, cw=cw, wi=wi: e.tensor_scalar(
               lbuf[:, 0:cw], s[:, 0:cw], thrn[:, wi:wi + 1], None,
               ALU.is_le), V)
        ve([(T, q_state["mm"][qb])],
           lambda e, s=wld, q=qstage[qb], cw=cw, wi=wi:
           e.scalar_tensor_tensor(q[:, 0:cw], s[:, 0:cw],
                                  thr[:, wi:wi + 1], lbuf[:, 0:cw],
                                  ALU.is_ge, ALU.subtract), V)
        q_state["cons"][b] = V.n
        stt_v = V.n
        for g in range(cw // 512):
            bank = 2 + q_state["tp"] % 2
            tb = q_state["tp"] % 3
            q_state["tp"] += 1
            tw = [(V, stt_v)] + (bank_free[bank] or []) + consts_toks
            for i in range(4):
                te(tw if i == 0 else [],
                   lambda e, bk=bank, q=qstage[qb], g=g, i=i: e.matmul(
                       ps[bk][:, i * 128:(i + 1) * 128],
                       q[:, g * 512 + i * 128: g * 512 + (i + 1) * 128],
                       ident, start=True, stop=True),
                   T if i == 3 else None)
            mm = T.n
            q_state["mm"][qb] = mm
            se([(T, mm)] + ([q_state["tp_cons"][tb]]
                            if q_state["tp_cons"][tb] else []),
               lambda e, bk=bank, tb=tb: e.copy(tpstage[tb], ps[bk][:]), C)
            bank_free[bank] = [(C, C.n)]
            if wi < 2:
                wTd = w1T_h if wi == 0 else w2T_h
                stok = dma("scalar", [(C, C.n)],
                           lambda e, d=wTd, t=t, c0=c0, g=g, tb=tb:
                           e.dma_start(
                               d[t][:, c0 + g * 512: c0 + (g + 1) * 512],
                               tpstage[tb]))
            else:
                ccb = (c0 + g * 512) // 128
                zoff = (t % 4) * 128
                stok = dma(
                    "scalar", [(C, C.n)],
                    lambda e, t=t, ccb=ccb, zoff=zoff, tb=tb: e.dma_start(
                        w3T_h[t // 4].rearrange("p (cc z) -> p cc z", z=512)[
                            :, ccb:ccb + 4, zoff:zoff + 128],
                        tpstage[tb].rearrange("p (i z) -> p i z", i=4)))
            q_state["tp_cons"][tb] = stok
            q_state["store_toks"][wi].append(stok)
        return True

    def emit_prep_light():
        if emit_gamma_step():
            emit_gamma_step()
            return True
        return False

    def emit_prep_step():
        if emit_prep_light():
            return True
        for wi in range(3):
            if (wi in g_state["done_v"] and not thr_state["done"][wi]):
                emit_thr_finalize(wi)
                thr_state["done"][wi] = True
                return True
        return emit_quant_step()

    fwl = [BF[:, 63616 + i * 2048: 63616 + (i + 1) * 2048].bitcast(F32)
           for i in range(4)]
    fwl_cons = [0, 0, 0, 0]
    fwl_toks = []
    wst_cons = [0, 0, 0, 0]
    wT_gate = {w: {} for w in ("q", "k", "v", "o")}
    units = [(w, t) for w in ("q", "k", "v", "o") for t in range(16)]
    for u, (w, t) in enumerate(units):
        cb_ = u % 4
        src = win["w" + w]
        for hf in range(2):
            fb = (2 * u + hf) % 4
            ltok = dma("sync" if hf == 0 else "gpsimd",
                       [(V, fwl_cons[fb])],
                       lambda e, d=fwl[fb], s=src, t=t, hf=hf: e.dma_start(
                           d, s[t * 128:(t + 1) * 128,
                                hf * 1024:(hf + 1) * 1024]))
            fwl_toks.append(ltok)
            ve([ltok, (T, wst_cons[cb_])],
               lambda e, d=wstream[cb_], s=fwl[fb], hf=hf: e.tensor_copy(
                   d[:, hf * 1024:(hf + 1) * 1024], s), V)
            fwl_cons[fb] = V.n
        cast_v = V.n
        toks = []
        for g in range(4):
            bank = 2 + g % 2
            tb = q_state["tp"] % 3
            q_state["tp"] += 1
            tw = [(V, cast_v)] + (bank_free[bank] or []) + consts_toks
            for i in range(4):
                te(tw if i == 0 else [],
                   lambda e, bk=bank, s=wstream[cb_], g=g, i=i: e.matmul(
                       ps[bk][:, i * 128:(i + 1) * 128],
                       s[:, (g * 4 + i) * 128:(g * 4 + i + 1) * 128],
                       ident, start=True, stop=True),
                   T if i == 3 else None)
            mm = T.n
            se([(T, mm)] + ([q_state["tp_cons"][tb]]
                            if q_state["tp_cons"][tb] else []),
               lambda e, bk=bank, tb=tb: e.copy(tpstage[tb], ps[bk][:]), C)
            bank_free[bank] = [(C, C.n)]
            if w != "o":
                stok = dma("scalar", [(C, C.n)],
                           lambda e, w=w, t=t, g=g, tb=tb: e.dma_start(
                               wT[w][t][:, g * 512:(g + 1) * 512],
                               tpstage[tb]))
            else:
                stok = dma("scalar", [(C, C.n)],
                           lambda e, t=t, g=g, tb=tb: e.dma_start(
                               woT_h[4 * g:4 * g + 4][
                                   :, :, t * 128:(t + 1) * 128].rearrange(
                                   "j p c -> p j c"),
                               tpstage[tb].rearrange("p (j c) -> p j c",
                                                     j=4)))
            q_state["tp_cons"][tb] = stok
            toks.append(stok)
        wst_cons[cb_] = T.n
        wT_gate[w][t] = toks
    wunit_state = {"i": len(units)}

    if NPHASES <= 1:
        ops_end(nc, refs, ops, Q_S, Q_C, Q_G, se, emit)
        return nc

    x_cons = [0, 0]
    ln1b_cons = [None, None]
    ln1T_cons = [0, 0]
    A_half_tok = [[None, None] for _ in range(NCH)]
    last_exp = [0, 0]
    last_rec = [0, 0]
    nmx_v = [0, 0, 0, 0]
    last_aT_mm = [0, 0]
    last_out_mm = [0, 0]
    ln1T_ready = [None, None]
    ln1_store_all = []

    def emit_ln1_chunk(c):
        pos0 = c * CH
        ln1_store_toks = []
        for i in range(4):
            xb = xbuf[i % 2]
            r0 = pos0 + i * 128
            xtok = dma("sync", [(V, x_cons[i % 2])],
                       lambda e, d=xb, r0=r0: e.dma_start(
                           d, x_h[r0:r0 + 128, :]))
            r_done = emit_ln(xb, 2048, trashA, extra_v=[xtok],
                             extra_c=[xtok])
            lb = ln1b[i % 2]
            ve([(C, r_done)] + ([ln1b_cons[i % 2]] if ln1b_cons[i % 2]
                                else []),
               lambda e, s=xb, d=lb: e.tensor_scalar(
                   d, s, ST0["m"], ST0["r"], ALU.subtract, ALU.mult), V)
            x_cons[i % 2] = V.n
            stok = dma("scalar", [(V, V.n)],
                       lambda e, s=lb, r0=r0: e.dma_start(
                           ln1_hbm[r0:r0 + 128, :], s))
            ln1b_cons[i % 2] = stok
            ln1_store_toks.append(stok)
            ln1_store_all.append(stok)
        ln1_gate = latest(ln1_store_toks)
        par = c % 2
        lt = ln1T[par]
        lnT_toks = []
        for Dc in range(16):
            lnT_toks.append(dma(
                "sync", ln1_gate + [(T, ln1T_cons[par])],
                lambda e, d=lt, Dc=Dc, p0=pos0: e.dma_start(
                    d[:, Dc * 512:(Dc + 1) * 512],
                    ln1_hbm[p0:p0 + CH, Dc * 128:(Dc + 1) * 128],
                    transpose=True)))
        ln1T_ready[par] = latest(lnT_toks)

    emit_ln1_chunk(0)
    for c in range(NCH):
        par = c % 2
        lt = ln1T[par]

        for pi, p in enumerate(("q", "k", "v")):
            ename = "scalar"
            for h in range(H):
                gi = pi * H + h
                rbuf = gi % 4
                wtok = dma("sync", [(T, wst_cons[rbuf])] + wT_gate[p][h],
                           lambda e, d=wstream[rbuf], p=p, h=h: e.dma_start(
                               d, wT[p][h]))
                bank = gi % 2
                tw = [wtok] + ln1T_ready[par] + (bank_free[bank] or [])
                for Dc in range(16):
                    te(tw if Dc == 0 else [],
                       lambda e, b=bank, wl=wstream[rbuf], lt=lt, Dc=Dc:
                       e.matmul(ps[b][:], wl[:, Dc * 128:(Dc + 1) * 128],
                                lt[:, Dc * 512:(Dc + 1) * 512],
                                start=(Dc == 0), stop=(Dc == 15)),
                       T if Dc == 15 else None)
                wst_cons[rbuf] = T.n
                mm_done = T.n
                if p == "q":
                    dst = QT.rearrange("p (g s h) -> p g s h", g=64, s=8)[
                        :, :, :, h]
                    srcp = ps[bank][:].rearrange("p (g s) -> p g s", g=64)
                    se([(T, mm_done)],
                       lambda e, d=dst, s=srcp: e.activation(
                           d, s, ACT.Copy, scale=c_qs), C)
                elif p == "k":
                    dst = KT.rearrange("p (g t s) -> p g t s", g=64, t=16)[
                        :, :, h, :]
                    srcp = ps[bank][:].rearrange("p (g s) -> p g s", g=64)
                    se([(T, mm_done)],
                       lambda e, d=dst, s=srcp: e.activation(d, s, ACT.Copy),
                       C)
                else:
                    dst = VT.rearrange("p (g t s) -> p g t s", g=64, t=16)[
                        :, :, h, :]
                    srcp = ps[bank][:].rearrange("p (g s) -> p g s", g=64)
                    se([(T, mm_done)],
                       lambda e, d=dst, s=srcp: e.activation(d, s, ACT.Copy),
                       C)
                bank_free[bank] = [(ECTR[ename], ECTR[ename].n)]
                emit_prep_step()
        ln1T_cons[par] = T.n
        qkv_done = [(V, V.n), (C, C.n)]

        for g4 in range(16):
            bank = 2 + (g4 % 2)
            tw = qkv_done + (bank_free[bank] or []) + consts_toks
            for i in range(4):
                g8 = 4 * g4 + i
                te(tw if i == 0 else [],
                   lambda e, b=bank, i=i, g8=g8: e.matmul(
                       ps[b][:, i * 128:(i + 1) * 128],
                       VT[:, g8 * 128:(g8 + 1) * 128], ident,
                       start=True, stop=True),
                   T if i == 3 else None)
            ve([(T, T.n)], lambda e, b=bank, g4=g4: e.tensor_copy(
                VT[:, g4 * 512:(g4 + 1) * 512], ps[b][:]), V)
            bank_free[bank] = [(V, V.n)]
            emit_prep_step()
        vgrp_done = V.n

        if c + 1 < NCH:
            emit_ln1_chunk(c + 1)

        def a5_stage_a(it):
            for hv in range(2):
                g4 = 2 * it + hv
                sbk = (4, 7)[hv] if it % 2 == 0 else (5, 3)[hv]
                tw = [(V, vgrp_done)] + (bank_free[sbk] or []) + consts_toks
                te(tw, lambda e, b=sbk: e.matmul(
                    ps[b][:], ident, mask4bf, start=True, stop=False), None)
                for i in range(4):
                    g8 = 4 * g4 + i
                    te([], lambda e, b=sbk, i=i, g8=g8: e.matmul(
                        ps[b][:, i * 128:(i + 1) * 128],
                        QT[:, g8 * 128:(g8 + 1) * 128],
                        KT[:, g8 * 128:(g8 + 1) * 128],
                        start=False, stop=(i == 3)),
                       T if i == 3 else None)
                sc_mm = T.n
                nm = nmx4[2 * (it % 2) + hv]
                ve([(T, sc_mm), (C, last_exp[hv])],
                   lambda e, b=sbk, d=nm: e.tensor_reduce(
                       d, ps[b][:].rearrange("p (g cc) -> p g cc", g=4),
                       AX.X, ALU.max, negate=True), V)
                nmx_v[2 * (it % 2) + hv] = V.n

        def a5_stage_b(it):
            nonlocal_A = [0]
            for hv in range(2):
                g4 = 2 * it + hv
                sbk = (4, 7)[hv] if it % 2 == 0 else (5, 3)[hv]
                nm = nmx4[2 * (it % 2) + hv]
                ex = expS[hv]
                su = sums4[hv]
                for g in range(4):
                    se([(V, nmx_v[2 * (it % 2) + hv]), (T, last_aT_mm[hv]),
                        (V, last_rec[hv])] if g == 0 else [],
                       lambda e, b=sbk, g=g, ex=ex, nm=nm, su=su:
                       e.activation(ex[:, g * 128:(g + 1) * 128],
                                    ps[b][:, g * 128:(g + 1) * 128],
                                    ACT.Exp, bias=nm[:, g:g + 1],
                                    accum_out=su[:, g:g + 1]), C)
                last_exp[hv] = C.n
                bank_free[sbk] = [(C, C.n)]
                rc = rec4[hv]
                ve([(C, last_exp[hv])], lambda e, s=su, d=rc: e.reciprocal(
                    d, s), V)
                last_rec[hv] = V.n
                dg = diag4[hv]
                ve([(T, last_aT_mm[hv])],
                   lambda e, d=dg, r=rc: e.tensor_tensor(
                       d.rearrange("p (g cc) -> p g cc", g=4),
                       ident4.rearrange("p (g cc) -> p g cc", g=4),
                       r.broadcast_to([128, 4, 128]), ALU.mult), V)
                diag_done = V.n
                at = attnT_sb[hv]
                tw = [(V, diag_done)] + (bank_free[6] or [])
                for i in range(4):
                    te(tw if i == 0 else [],
                       lambda e, i=i, s=ex, d=dg: e.matmul(
                           ps[6][:, i * 128:(i + 1) * 128],
                           s[:, i * 128:(i + 1) * 128],
                           d[:, i * 128:(i + 1) * 128],
                           start=True, stop=True),
                       T if i == 3 else None)
                last_aT_mm[hv] = T.n
                se([(T, last_aT_mm[hv]), (T, last_out_mm[hv])],
                   lambda e, d=at: e.copy(d, ps[6][:]), C)
                bank_free[6] = [(C, C.n)]
                aT_evac = C.n
                tw = [(C, aT_evac)] + (bank_free[2] or [])
                for i in range(4):
                    g8 = 4 * g4 + i
                    te(tw if i == 0 else [],
                       lambda e, i=i, g8=g8, s=at: e.matmul(
                           ps[2][:, i * 128:(i + 1) * 128],
                           VT[:, g8 * 128:(g8 + 1) * 128],
                           s[:, i * 128:(i + 1) * 128],
                           start=True, stop=True),
                       T if i == 3 else None)
                last_out_mm[hv] = T.n
                war = []
                if c >= 1:
                    war = A_half_tok[c - 1][0 if it < 4 else 1]
                for parx in range(2):
                    srcA = ps[2][:].rearrange("p (i a s h) -> p i a s h",
                                              i=2, a=2, s=8)[:, :, parx, :, :]
                    dstA = Abuf.rearrange("p (a s g h) -> p a s g h",
                                          a=2, s=8, g=32)[
                        :, parx, :, 2 * g4: 2 * g4 + 2, :].rearrange(
                        "p s g h -> p g s h")
                    ve([(T, last_out_mm[hv])] + war,
                       lambda e, s=srcA, d=dstA: e.tensor_copy(d, s), V)
                bank_free[2] = [(V, V.n)]
                nonlocal_A[0] = V.n
            emit_prep_light()
            if it == 3 or it == 7:
                hb = 0 if it == 3 else 1
                toks = []
                for bb in (2 * hb, 2 * hb + 1):
                    toks.append(dma(
                        "scalar", [(V, nonlocal_A[0])],
                        lambda e, c=c, bb=bb: e.dma_start(
                            A_hbm[4 * c + bb].rearrange(
                                "p (j cc) -> p j cc", j=16),
                            Abuf.rearrange("p (j b cc) -> p j b cc",
                                           j=16, b=4)[:, :, bb, :])))
                A_half_tok[c][hb] = latest(toks)

        for it in range(9):
            if it < 8:
                a5_stage_a(it)
            if it >= 1:
                a5_stage_b(it - 1)

    while emit_prep_step():
        pass
    assert g_state["i"] == len(g_tiles) and q_state["i"] == len(q_tiles)
    w1T_gate = latest(q_state["store_toks"][0])
    w2T_gate = latest(q_state["store_toks"][1])
    w3T_gate = latest(q_state["store_toks"][2])
    woT_gate = latest([t for h in range(16) for t in wT_gate["o"][h]])

    if NPHASES <= 2:
        ops_end(nc, refs, ops, Q_S, Q_C, Q_G, se, emit)
        return nc

    hid = BF[:, 640:23168]
    lnT = [BF[:, 23168 + i * 8192: 23168 + (i + 1) * 8192] for i in range(2)]
    wpool = [BF[:, 39552 + i * 2048: 39552 + (i + 1) * 2048]
             for i in range(4)]
    fstage = BF[:, 47744:55936]
    Aslab = [BF[:, 55936 + i * 2048: 55936 + (i + 1) * 2048]
             for i in range(4)]
    Ostb = [BF[:, 64128 + i * 2048: 64128 + (i + 1) * 2048] for i in range(4)]
    ln3b = [BF[:, 72832 + i * 2048: 72832 + (i + 1) * 2048] for i in range(2)]
    wostrip = [BF[:, 76928:77440], BF[:, 77440:77952],
               BF[:, 77952:78464], BF[:, 72320:72832]]
    xg = [xbuf[0], xbuf[1],
          FA[:, 6400:8448], FA[:, 8448:10496]]
    sigp = [FA[:, 4864:5376], FA[:, 5376:5888]]
    trashF = trashA

    snapA = ([(T, T.n), (V, V.n), (C, C.n)]
             + Q_S.tokens() + Q_C.tokens() + Q_G.tokens())

    strip_cons = [0, 0, 0, 0]
    group_mm_done = [0]
    aslab_tok = [None] * 4
    xg_tok = [None] * 4
    xg_cons = [[], [], [], []]
    ostb_cons = [[], [], [], []]
    ostb_evac = [None] * 4
    s1_trash_c = [0]
    x2_store_tok = [None] * 32
    ln3b_cons = [None, None]
    lnT_cons = [0, 0]
    lnT_fill = [[], []]
    pool_cons = [0, 0, 0, 0]
    fst_cons = [0, 0, 0, 0]
    hid_war = [0]
    ln4_prev_v = [0]
    s2_stt = [0] * 4
    s2_ln3c = [0] * 4

    def o_group_loads(g):
        for cbq in range(4):
            cb = 4 * g + cbq
            tk = [(C, s1_trash_c[0])] if cbq == 0 else []
            aslab_tok[cbq] = dma(
                "scalar",
                snapA + tk + A_half_tok[g][0 if cbq < 2 else 1]
                + [(T, group_mm_done[0])],
                lambda e, d=Aslab[cbq], cb=cb: e.dma_start(d, A_hbm[cb]))
        for sl in range(2):
            cb = 4 * g + sl
            xg_tok[sl] = dma(
                "scalar", snapA + xg_cons[sl],
                lambda e, d=xg[sl], cb=cb: e.dma_start(
                    d, x_h.rearrange("(h g) dd -> g h dd", h=16)[
                        8 * cb: 8 * cb + 8]))

    def o_group_loads2(g):
        for sl in (2, 3):
            cb = 4 * g + sl
            xg_tok[sl] = dma(
                "scalar", snapA + xg_cons[sl],
                lambda e, d=xg[sl], cb=cb: e.dma_start(
                    d, x_h.rearrange("(h g) dd -> g h dd", h=16)[
                        8 * cb: 8 * cb + 8]))

    def o_pass(g, dc):
        mm_done = [0] * 4
        for j in range(16):
            sl = j % 4
            eng = "gpsimd" if j % 2 == 0 else "sync"
            stok = dma(eng, snapA + [(T, strip_cons[sl])] + woT_gate,
                       lambda e, d=wostrip[sl], j=j, dc=dc: e.dma_start(
                           d, woT_h[j][:, dc * 512:(dc + 1) * 512]))
            for cbq in range(4):
                tw = []
                if j == 0:
                    tw = [aslab_tok[cbq]] + (bank_free[cbq] or [])
                if cbq == 0:
                    tw = tw + [stok]
                do_inc = (cbq == 3) or (j == 15)
                te(tw, lambda e, cbq=cbq, sl=sl, j=j: e.matmul(
                    ps[cbq][:], Aslab[cbq][:, j * 128:(j + 1) * 128],
                    wostrip[sl], start=(j == 0), stop=(j == 15)),
                   T if do_inc else None)
                if cbq == 3:
                    strip_cons[sl] = T.n
                if j == 15:
                    mm_done[cbq] = T.n
        for cbq in range(4):
            eng = "scalar" if cbq < 2 else "vector"
            ww = [(T, mm_done[cbq])]
            if dc == 0:
                ww = ww + ostb_cons[cbq] + snapA
            EMIT[eng](ww,
                      lambda e, cbq=cbq, dc=dc, eng=eng:
                      (e.copy if eng == "scalar" else e.tensor_copy)(
                          Ostb[cbq][:, dc * 512:(dc + 1) * 512],
                          ps[cbq][:]), ECTR[eng])
            bank_free[cbq] = [(ECTR[eng], ECTR[eng].n)]
            ostb_evac[cbq] = (ECTR[eng], ECTR[eng].n)
        group_mm_done[0] = T.n

    def o_s1(g, cbq):
        v_done, c_done = emit_ln_v(Ostb[cbq], cbq, Aslab[0][:, 0:2048],
                                   extra_v=[ostb_evac[cbq], (C, C.n)],
                                   extra_c=[ostb_evac[cbq], (V, V.n)])
        s1_trash_c[0] = c_done
        return c_done

    def o_s2(g, cbq):
        st = stat(cbq)
        bsl = cbq % 2
        cb = 4 * g + cbq
        ve([(C, C.n)], lambda e, st=st: e.tensor_scalar(
            st["mr"], st["m"], st["r"], None, ALU.mult), V)
        ve([xg_tok[cbq]], lambda e, st=st, cbq=cbq: e.tensor_scalar(
            xg[cbq], xg[cbq], st["mr"], None, ALU.subtract), V)
        ve([], lambda e, st=st, cbq=cbq: e.scalar_tensor_tensor(
            xg[cbq], Ostb[cbq], st["r"], xg[cbq], ALU.mult, ALU.add), V)
        s2_stt[cbq] = V.n
        stok = dma("scalar", [(V, V.n)],
                   lambda e, cbq=cbq, cb=cb: e.dma_start(
                       x2_hbm[cb * 128:(cb + 1) * 128, :], xg[cbq]))
        x2_store_tok[cb] = stok
        v3, c3 = emit_ln_v(xg[cbq], cbq, Aslab[0][:, 0:2048],
                           extra_c=[(V, V.n)])
        s1_trash_c[0] = c3
        s2_ln3c[cbq] = c3
        ve([(C, c3)] + snapA + ([ln3b_cons[bsl]] if ln3b_cons[bsl] else []),
           lambda e, st=st, bsl=bsl, cbq=cbq: e.tensor_scalar(
               ln3b[bsl], xg[cbq], st["m"], st["r"],
               ALU.subtract, ALU.mult), V)
        xg_cons[cbq] = [stok, (V, V.n), (C, c3)]
        ostb_cons[cbq] = [(V, s2_stt[cbq]), (C, c3)]

    def lnT_transpose(g, cbq):
        sl = cbq % 2
        dst = lnT[g % 2]
        if cbq == 0:
            lnT_fill[g % 2] = []
        for eg in range(4):
            bank = eg % 2
            tw = [(V, V.n)] + (bank_free[bank] or []) + consts_toks
            for i in range(4):
                te(tw if i == 0 else [],
                   lambda e, bk=bank, sl=sl, eg=eg, i=i: e.matmul(
                       ps[bk][:, i * 128:(i + 1) * 128],
                       ln3b[sl][:, (eg * 4 + i) * 128:(eg * 4 + i + 1) * 128],
                       ident, start=True, stop=True),
                   T if i == 3 else None)
            mm = T.n
            eng = "vector" if eg % 2 == 0 else "scalar"
            dste = dst.rearrange("p (Dc z) -> p Dc z", Dc=16)[
                :, eg * 4:(eg + 1) * 4, cbq * 128:(cbq + 1) * 128]
            EMIT[eng]([(T, mm), (T, lnT_cons[g % 2])] + snapA,
                      lambda e, d=dste, bk=bank, eng=eng:
                      (e.copy if eng == "scalar" else e.tensor_copy)(
                          d, ps[bk][:].rearrange("p (i z) -> p i z", i=4)),
                      ECTR[eng])
            bank_free[bank] = [(ECTR[eng], ECTR[eng].n)]
            lnT_fill[g % 2].append((ECTR[eng], ECTR[eng].n))
        ln3b_cons[sl] = (T, mm)

    def hidden_section(r, cp_lo, cp_hi):
        for cp in range(cp_lo, cp_hi):
            s1 = cp % 2
            s2 = 2 + cp % 2
            w1_ld = dma("sync", snapA + [(T, pool_cons[s1])] + w1T_gate,
                        lambda e, cp=cp, s1=s1: e.dma_start(
                            wpool[s1], w1T_h[cp]))
            w2_ld = dma("sync", snapA + [(T, pool_cons[s2])] + w2T_gate,
                        lambda e, cp=cp, s2=s2: e.dma_start(
                            wpool[s2], w2T_h[cp]))
            b1 = cp % 2
            b2 = 2 + cp % 2
            lt = lnT[r % 2]
            tw = [w1_ld] + (bank_free[b1] or [])
            if cp == cp_lo:
                tw = tw + latest(lnT_fill[r % 2])
            for Dc in range(16):
                te(tw if Dc == 0 else [],
                   lambda e, b=b1, s1=s1, lt=lt, Dc=Dc: e.matmul(
                       ps[b][:], wpool[s1][:, Dc * 128:(Dc + 1) * 128],
                       lt[:, Dc * 512:(Dc + 1) * 512],
                       start=(Dc == 0), stop=(Dc == 15)),
                   T if Dc == 15 else None)
            x1_mm = T.n
            tw = [w2_ld] + (bank_free[b2] or [])
            for Dc in range(16):
                te(tw if Dc == 0 else [],
                   lambda e, b=b2, s2=s2, lt=lt, Dc=Dc: e.matmul(
                       ps[b][:], wpool[s2][:, Dc * 128:(Dc + 1) * 128],
                       lt[:, Dc * 512:(Dc + 1) * 512],
                       start=(Dc == 0), stop=(Dc == 15)),
                   T if Dc == 15 else None)
            x2_mm = T.n
            pool_cons[s1] = x1_mm
            pool_cons[s2] = x2_mm
            sg = sigp[cp % 2]
            se([(T, x1_mm), (V, sig_war[cp % 2])],
               lambda e, b=b1, sg=sg: e.activation(sg, ps[b][:],
                                                   ACT.Sigmoid), C)
            bank_free[b1] = [(C, C.n)]
            mw = [(C, C.n), (T, x2_mm)]
            if cp == 0:
                mw = mw + [(T, hid_war[0])] + latest(ln1_store_all)
            ve(mw, lambda e, b=b2, cp=cp, sg=sg: e.tensor_tensor(
                hid[:, cp * 512:(cp + 1) * 512], ps[b][:], sg,
                ALU.mult), V)
            nonlocal_sig[0] = V.n
            sig_war[cp % 2] = V.n
            bank_free[b2] = [(V, V.n)]

    nonlocal_sig = [0]
    sig_war = [0, 0]

    def out_sweep(r, dc):
        hid_done = nonlocal_sig[0]
        mm_of = [0] * 4
        for grp in range(11):
            sl = grp % 4
            eng = "scalar" if grp % 2 == 0 else "sync"
            w3tok = dma(eng, snapA + [(T, pool_cons[sl])] + w3T_gate,
                        lambda e, sl=sl, dc=dc, grp=grp: e.dma_start(
                            wpool[sl],
                            w3T_h[dc][:, grp * 4 * 512:(grp + 1) * 4 * 512]))
            for ccl in range(4):
                cc = grp * 4 + ccl
                for posb in range(4):
                    bank = 4 + posb
                    tw = []
                    if cc == 0:
                        tw = [(V, hid_done)] + (bank_free[bank] or [])
                    if ccl == 0 and posb == 0:
                        tw = tw + [w3tok]
                    do_inc = (cc == 43) or (posb == 3 and ccl == 3)
                    te(tw, lambda e, b=bank, sl=sl, ccl=ccl, cc=cc,
                       posb=posb: e.matmul(
                           ps[b][:],
                           hid[:, cc * 512 + posb * 128:
                               cc * 512 + (posb + 1) * 128],
                           wpool[sl][:, ccl * 512:(ccl + 1) * 512],
                           start=(cc == 0), stop=(cc == 43)),
                       T if do_inc else None)
                    if posb == 3 and ccl == 3:
                        pool_cons[sl] = T.n
                    if cc == 43:
                        mm_of[posb] = T.n
        for posb in range(4):
            bank = 4 + posb
            eng = "scalar" if posb % 2 == 0 else "vector"
            fsl = fstage[:, posb * 2048 + dc * 512:
                         posb * 2048 + (dc + 1) * 512]
            EMIT[eng]([(T, mm_of[posb]), (V, fst_cons[posb])] + snapA,
                      lambda e, b=bank, fsl=fsl, eng=eng:
                      (e.copy if eng == "scalar" else e.tensor_copy)(
                          fsl, ps[b][:]), ECTR[eng])
            bank_free[bank] = [(ECTR[eng], ECTR[eng].n)]
            evac_tok = (ECTR[eng], ECTR[eng].n)
            scol = ln4_scol + posb * 4 + dc
            qcol = ln4_qcol + posb * 4 + dc
            ve([evac_tok, (V, ln4_prev_v[0])],
               lambda e, fsl=fsl, scol=scol: e.tensor_reduce(
                   gtmp[:, scol:scol + 1], fsl, AX.X, ALU.add), V)
            se([evac_tok, (V, ln4_prev_v[0])],
               lambda e, fsl=fsl, qcol=qcol: e.activation(
                   trashF, fsl, ACT.Square,
                   accum_out=gtmp[:, qcol:qcol + 1]), C)
        return mm_of[3]

    def ln4_store(r, posb):
        sl = 2 + posb % 2
        cb = 4 * r + posb
        xtok = dma("gpsimd",
                   snapA + xg_cons[sl] + [x2_store_tok[cb]],
                   lambda e, d=xg[sl], cb=cb: e.dma_start(
                       d, x2_hbm[cb * 128:(cb + 1) * 128, :]))
        ve([(C, C.n)], lambda e, posb=posb: e.tensor_reduce(
            ST0["sum"], gtmp[:, ln4_scol + posb * 4: ln4_scol + posb * 4 + 4],
            AX.X, ALU.add), V)
        ve([], lambda e, posb=posb: e.tensor_reduce(
            ST0["sq"], gtmp[:, ln4_qcol + posb * 4: ln4_qcol + posb * 4 + 4],
            AX.X, ALU.add), V)
        ve([], lambda e: e.tensor_scalar(ST0["m"], ST0["sum"], 1.0 / D, None,
                                         ALU.mult), V)
        ve([], lambda e: e.tensor_scalar(ST0["t1"], ST0["m"], ST0["m"], None,
                                         ALU.mult), V)
        ve([], lambda e: e.tensor_scalar(ST0["nv"], ST0["t1"], -1.0, EPS_LN,
                                         ALU.mult, ALU.add), V)
        ve([], lambda e: e.tensor_scalar(ST0["msq"], ST0["sq"], 1.0 / D, None,
                                         ALU.mult), V)
        ve([], lambda e: e.tensor_tensor(ST0["t1"], ST0["msq"], ST0["nv"],
                                         ALU.add), V)
        ve([], lambda e: e.reciprocal(ST0["nv"], ST0["t1"]), V)
        se([(V, V.n)], lambda e: e.activation(
            ST0["r"], ST0["nv"], ACT.Sqrt), C)
        r4 = C.n
        ve([(C, r4)], lambda e: e.tensor_scalar(
            ST0["mr"], ST0["m"], ST0["r"], None, ALU.mult), V)
        ve([xtok], lambda e, sl=sl: e.tensor_scalar(
            xg[sl], xg[sl], ST0["mr"], None, ALU.subtract), V)
        fsl = fstage[:, posb * 2048:(posb + 1) * 2048]
        ve([], lambda e, sl=sl, fsl=fsl: e.scalar_tensor_tensor(
            xg[sl], fsl, ST0["r"], xg[sl], ALU.mult, ALU.add), V)
        fst_cons[posb] = V.n
        ln4_prev_v[0] = V.n
        stok = dma("scalar", [(V, V.n)],
                   lambda e, sl=sl, cb=cb: e.dma_start(
                       out_h.rearrange("(h g) dd -> g h dd", h=16)[
                           8 * cb: 8 * cb + 8], xg[sl]))
        xg_cons[sl] = [stok, (V, V.n)]

    o_group_loads(0)
    o_group_loads2(0)
    for dc in range(4):
        o_pass(0, dc)
    act_load("sqrt_and_friends")
    for cbq in range(4):
        o_s1(0, cbq)
    o_s2(0, 0)
    o_s2(0, 1)
    lnT_transpose(0, 0)
    lnT_transpose(0, 1)
    o_s2(0, 2)
    lnT_transpose(0, 2)
    o_s2(0, 3)
    lnT_transpose(0, 3)
    act_load("sigmoid_and_others")

    if NPHASES <= 3:
        ops_end(nc, refs, ops, Q_S, Q_C, Q_G, se, emit)
        return nc

    for r in range(NRB):
        gn = r + 1
        hidden_section(r, 0, 22)
        if gn < NRB:
            o_group_loads(gn)
            o_group_loads2(gn)
            for dc in range(4):
                o_pass(gn, dc)
        hidden_section(r, 22, 44)
        lnT_cons[r % 2] = T.n
        act_load("sqrt_and_friends")
        if gn < NRB:
            for cbq in range(4):
                o_s1(gn, cbq)
            o_s2(gn, 0)
            o_s2(gn, 1)
        out_sweep(r, 0)
        if gn < NRB:
            lnT_transpose(gn, 0)
            lnT_transpose(gn, 1)
            o_s2(gn, 2)
            o_s2(gn, 3)
        out_sweep(r, 1)
        if gn < NRB:
            lnT_transpose(gn, 2)
            lnT_transpose(gn, 3)
        out_sweep(r, 2)
        hid_war[0] = out_sweep(r, 3)
        for posb in range(4):
            ln4_store(r, posb)
        act_load("sigmoid_and_others")

    ops_end(nc, refs, ops, Q_S, Q_C, Q_G, se, emit)
    return nc


def _consts():
    eye = np.eye(128, dtype=ml_dtypes.bfloat16)
    ident4 = np.concatenate([eye] * 4, axis=1)
    p = np.arange(128)[:, None]
    cc = np.arange(128)[None, :]
    m = np.where((cc % 8) == (p // 16), 0.0, NEG).astype(np.float32)
    mask4 = np.concatenate([m] * 4, axis=1)
    return eye, ident4, mask4


def kernel(L0_attn_x, L0_attn_wq, L0_attn_wk, L0_attn_wv, L0_attn_wo,
           L0_ffn_w1, L0_ffn_w2, L0_ffn_w3):
    x = np.ascontiguousarray(np.asarray(L0_attn_x, dtype=np.float32))
    ws = {n: np.ascontiguousarray(np.asarray(a, dtype=np.float32))
          for n, a in (("wq", L0_attn_wq), ("wk", L0_attn_wk),
                       ("wv", L0_attn_wv), ("wo", L0_attn_wo),
                       ("w1", L0_ffn_w1), ("w2", L0_ffn_w2),
                       ("w3", L0_ffn_w3))}
    eye, ident4, mask4 = _consts()
    mask4b = mask4.astype(ml_dtypes.bfloat16)

    nc = build_nc(debug=DEBUG_DUMP)
    in_maps = []
    for b in range(NCORES):
        m = {"x": np.ascontiguousarray(x[b]), "ident": eye,
             "ident4": ident4, "mask4b": mask4b}
        m.update(ws)
        in_maps.append(m)
    res = run_bass_kernel_spmd(nc, in_maps, core_ids=list(range(NCORES)))
    out = np.stack([np.asarray(res.results[b]["out"], dtype=np.float32)
                    for b in range(NCORES)], axis=0)
    if DEBUG_DUMP:
        kernel._debug = res
    return out
